# version 02
# baseline (speedup 1.0000x reference)
import contextlib
import numpy as np
import concourse.bass as bass
import concourse.mybir as mybir
from concourse.bass_utils import run_bass_kernel_spmd

F32 = mybir.dt.float32
BF16 = mybir.dt.bfloat16
AF = mybir.ActivationFunctionType
ALU = mybir.AluOpType
AX = mybir.AxisListType

NCORES = 8
T = 4096
D = 1024
NT = T // 128
ND = D // 128
NQC = T // 512


class Sem:
    __slots__ = ("h", "n")
    def __init__(self, handle):
        self.h = handle
        self.n = 0
    def inc(self, k=1):
        self.n += k
        return self.n


class Rot:
    def __init__(self, bufs):
        self.bufs = bufs
        self.guard = [None] * len(bufs)
        self.i = 0
    def next(self):
        i = self.i % len(self.bufs)
        self.i += 1
        return self.bufs[i], i, self.guard[i]
    def release(self, i, sem, val):
        self.guard[i] = (sem, val)


class Prog:
    ENGINES = ("sync", "scalar", "vector", "tensor", "gpsimd")

    def __init__(self, name="prog"):
        self.nc = bass.Bass("TRN2", target_bir_lowering=False)
        self.es = contextlib.ExitStack()
        self.ops = {e: [] for e in self.ENGINES}
        self._semcnt = 0
        self._last_wait = {}

    def sem(self):
        self._semcnt += 1
        return Sem(self.es.enter_context(self.nc.semaphore(f"s{self._semcnt}")))

    def sbuf(self, name, shape, dt, es=None):
        return (es or self.es).enter_context(self.nc.sbuf_tensor(name, shape, dt))

    def psum(self, name, shape, dt=F32, es=None):
        return (es or self.es).enter_context(self.nc.psum_tensor(name, shape, dt))

    def on(self, eng, fn):
        self.ops[eng].append(fn)

    def wait(self, eng, sem, val=None):
        v = sem.n if val is None else val
        if v <= 0:
            return
        key = (eng, id(sem))
        if self._last_wait.get(key, 0) >= v:
            return
        self._last_wait[key] = v
        self.on(eng, lambda e, s=sem.h, v=v: e.wait_ge(s, v))

    def guard(self, eng, g):
        if g is not None:
            self.wait(eng, g[0], g[1])

    def dma(self, q, dst, src, sem):
        self.on(q, lambda e, d=dst, s=src, h=sem.h: e.dma_start(d, s).then_inc(h, 16))
        return sem.inc(16)

    def mm(self, out, lhsT, rhs, start, stop, sem=None, tile_position=None):
        def f(e, o=out, l=lhsT, r=rhs, st=start, sp=stop, s=sem, tp=tile_position):
            i = e.matmul(o, l, r, start=st, stop=sp, tile_position=tp)
            if s is not None:
                i.then_inc(s.h, 1)
        self.on("tensor", f)
        return sem.inc() if sem is not None else None

    def ev(self, eng, sem, fn):
        def f(e, fn=fn, h=sem.h):
            fn(e).then_inc(h, 1)
        self.on(eng, f)
        return sem.inc()

    def finalize(self):
        block = self.es.enter_context(self.nc.Block())
        def run(en):
            def body(eng):
                for fn in self.ops[en]:
                    fn(eng)
            return body
        block.sync(run("sync"))
        block.scalar(run("scalar"))
        block.vector(run("vector"))
        block.tensor(run("tensor"))
        block.gpsimd(run("gpsimd"))
        self.es.close()
        return self.nc



def host_consts():
    hs = 64
    i = np.arange(hs // 2, dtype=np.float64)
    theta = (1.0 / 10000.0) ** (2.0 * i / hs)
    t = np.arange(1, T + 1, dtype=np.float64)[:, None]
    ang = (t * theta)
    cs2 = np.zeros((64, T), dtype=np.float32)
    cs2[0:32] = np.cos(ang).T.astype(np.float32)
    cs2[32:64] = np.sin(ang).T.astype(np.float32)
    return {
        "cs2": cs2,
        "ident": np.eye(128, dtype=np.float32),
        "triu": np.triu(np.ones((128, 128), dtype=np.float32)),
        "ones": np.ones((1, 512), dtype=np.float32),
    }


def perm_rope_cols():
    perm = []
    for oc in range(ND):
        base = oc * 128
        for h in range(2):
            hb = base + h * 64
            perm.extend(range(hb, hb + 64, 2))
            perm.extend(range(hb + 1, hb + 64, 2))
    return np.array(perm, dtype=np.int64)




def host_consts():
    hs = 64
    i = np.arange(hs // 2, dtype=np.float64)
    theta = (1.0 / 10000.0) ** (2.0 * i / hs)
    t = np.arange(1, T + 1, dtype=np.float64)[:, None]
    ang = t * theta
    cos4 = np.tile(np.cos(ang).T.astype(np.float32), (4, 1))
    sin4 = np.tile(np.sin(ang).T.astype(np.float32), (4, 1))
    return {
        "cos4": cos4, "sin4": sin4,
        "ident": np.eye(128, dtype=np.float32),
        "triu": np.triu(np.ones((128, 128), dtype=np.float32)),
    }


def perm_rope_cols():
    perm = []
    for p in range(4):
        for par in (0, 1):
            for h in range(4 * p, 4 * p + 4):
                perm.extend(range(64 * h + par, 64 * h + 64, 2))
    return np.array(perm, dtype=np.int64)



def emit_negC(P, scratch, negC, mult, clamp, s_chain, s_extra, bc_ps, tr_ps, ident_f):
    v = P.ev("tensor", s_extra, lambda e: e.transpose(tr_ps[0:32, 0:128], scratch[:, 0:32], ident_f[:, :]))
    P.wait("vector", s_extra, v)
    v = P.ev("vector", s_chain, lambda e: e.tensor_reduce(
        scratch[0:1, 36:37], tr_ps[0:1, 0:128], axis=AX.X, op=ALU.max))
    P.wait("vector", s_chain, v)
    v = P.ev("vector", s_chain, lambda e: e.tensor_mul(scratch[0:1, 38:39], scratch[0:1, 36:37], scratch[0:1, 36:37]))
    P.wait("vector", s_chain, v)
    v = P.ev("vector", s_chain, lambda e, m=mult: e.tensor_scalar(
        scratch[0:1, 39:40], scratch[0:1, 38:39], float(m), 0.05, ALU.mult, ALU.add))
    P.wait("vector", s_chain, v)
    v = P.ev("vector", s_chain, lambda e, c=clamp: e.tensor_scalar(
        scratch[0:1, 40:41], scratch[0:1, 39:40], float(c), -1.0, ALU.min, ALU.mult))
    P.wait("tensor", s_chain, v)
    vm = P.ev("tensor", s_extra, lambda e, p=bc_ps: e.matmul(
        p[:, 0:1], P._ones_f32[0:1, 0:128], scratch[0:1, 40:41], start=True, stop=True))
    P.wait("vector", s_extra, vm)
    return P.ev("vector", s_chain, lambda e, p=bc_ps: e.tensor_copy(negC[:, :], p[:, 0:1]))


def emit_attention(P, cfg):
    nheads = cfg["nheads"]; causal = cfg["causal"]; dh = cfg["dh"]
    ktb, vaug, yatt = cfg["ktb"], cfg["vaug"], cfg["yatt"]
    negC = cfg["negC"]; scale = cfg["scale"]; triu_bf = cfg.get("triu_bf")
    s_mm = cfg["s_mm"]; s_exp = cfg["s_exp"]; s_pv = cfg["s_pv"]
    s_evc = cfg["s_evc"]; s_msk = cfg["s_msk"]
    tag = cfg["tag"]
    per_chunk = 128 // dh

    ps_s = Rot([P.psum(f"ps_s{i}_{tag}", [128, 512], F32) for i in range(2)])
    pt = Rot([P.sbuf(f"pt{i}_{tag}", [128, 512], BF16) for i in range(2)])
    ps_pv = P.psum(f"ps_pv_{tag}", [65, 512], F32)
    ev_f32 = P.sbuf(f"ev_f32_{tag}", [128, 512], F32)
    ystg = P.sbuf(f"ystg_{tag}", [64, 512], BF16)
    s_mv = cfg["s_mv"]

    yatt_guard = None
    cfg["_ystg_guard"] = [None]
    for qc in range(NQC):
        nkt = min(NT, (qc + 1) * 4) if causal else NT
        qch, qg = cfg["get_q"](qc)
        if yatt_guard is not None:
            P.guard("vector", yatt_guard)
        P.guard("tensor", qg)
        for h in range(nheads):
            c = h // per_chunk
            r = dh * (h % per_chunk)
            pv_first = True
            for kti in range(nkt):
                m = kti - qc * 4
                diag = causal and m >= 0
                pb, pi, g = ps_s.next()
                P.guard("tensor", g)
                P.mm(pb[:, :], ktb[c][r:r + dh, kti * 128:(kti + 1) * 128],
                     qch[c][r:r + dh, :], start=True, stop=True, sem=s_mm,
                     tile_position=(r, 0))
                mv = s_mm.n
                tb, ti, g2 = pt.next()
                P.guard("scalar", g2)
                P.wait("scalar", s_mm, mv)
                if diag:
                    if m > 0:
                        P.guard("vector", g2)
                        P.on("vector", lambda e, t=tb, m=m: e.memset(t[:, 0:m * 128], 0.0))
                        zv = P.ev("vector", s_msk, lambda e: e.engine_nop())
                    else:
                        zv = None
                    ev = P.ev("scalar", s_exp, lambda e, t=tb, p=pb, m=m, b=negC, sc=scale:
                              e.activation(t[:, m * 128:512], p[:, m * 128:512], AF.Exp,
                                           bias=b[:, :], scale=sc))
                    P.wait("vector", s_exp, ev)
                    mk = P.ev("vector", s_msk, lambda e, t=tb, m=m, u=triu_bf:
                              e.tensor_mul(t[:, m * 128:(m + 1) * 128],
                                           t[:, m * 128:(m + 1) * 128], u[:, :]))
                    P.wait("tensor", s_msk, mk)
                    if zv is not None:
                        P.wait("tensor", s_msk, zv)
                else:
                    ev = P.ev("scalar", s_exp, lambda e, t=tb, p=pb, b=negC, sc=scale:
                              e.activation(t[:, :], p[:, :], AF.Exp, bias=b[:, :], scale=sc))
                    P.wait("tensor", s_exp, ev)
                ps_s.release(pi, s_exp, s_exp.n)
                hv = 65 * h
                P.mm(ps_pv[:, :], vaug[kti][:, hv:hv + 65], tb[:, :],
                     start=pv_first, stop=(kti == nkt - 1), sem=s_pv)
                pt.release(ti, s_pv, s_pv.n)
                pv_first = False
            pvv = s_pv.n
            P.wait("scalar", s_pv, pvv)
            e1 = P.ev("scalar", s_evc, lambda e, p=ps_pv: e.copy(ev_f32[0:64, :], p[0:64, :]))
            P.wait("vector", s_pv, pvv)
            e2 = P.ev("vector", s_evc, lambda e, p=ps_pv: e.reciprocal(ev_f32[64:65, :], p[64:65, :]))
            P.wait("tensor", s_evc, e2)
            bb, bi, g3 = ps_s.next()
            P.guard("tensor", g3)
            e3 = P.ev("tensor", s_evc, lambda e, p=bb: e.matmul(
                p[0:64, :], P._ones_f32[64:65, 0:64], ev_f32[64:65, :],
                start=True, stop=True, tile_position=(64, 0)))
            P.wait("vector", s_evc, e3)
            P.wait("vector", s_evc, e1)
            if h % 2 == 0:
                e4 = P.ev("vector", s_evc, lambda e, c_=h // 2, p=bb:
                          e.tensor_mul(yatt[c_][0:64, :], ev_f32[0:64, :], p[0:64, :]))
            else:
                P.guard("vector", cfg["_ystg_guard"][0])
                e4 = P.ev("vector", s_evc, lambda e, p=bb:
                          e.tensor_mul(ystg[:, :], ev_f32[0:64, :], p[0:64, :]))
                P.wait("gpsimd", s_evc, e4)
                mvv = P.dma("gpsimd", yatt[h // 2][64:128, :], ystg[:, :], s_mv)
                cfg["_ystg_guard"][0] = (s_mv, mvv)
            ps_s.release(bi, s_evc, e4)
            P.wait("tensor", s_evc, e2)
        yatt_guard = cfg["after_qc"](qc, s_evc.n, s_mv.n)


def emit_postln(P, ctx, tt, y_blk_ap, x_src_dram, y_dst_dram):
    xr_rot, lnt, ps_sq = ctx["xr_rot"], ctx["lnt"], ctx["ps_sq"]
    s_ld2, s_ln, s_st = ctx["s_ld2"], ctx["s_ln"], ctx["s_st"]
    D_ = D
    xr, xri, g = xr_rot.next()
    P.guard("gpsimd", g)
    v = P.dma("gpsimd", xr[:, :], x_src_dram[tt * 128:(tt + 1) * 128, :], s_ld2)
    P.wait("vector", s_ld2, v)
    av = P.ev("vector", s_ln, lambda e, x=xr, y=y_blk_ap: e.tensor_add(x[:, :], x[:, :], y))
    lt = lnt[tt % 2]
    P.wait("scalar", s_ln, av)
    sq1 = P.ev("scalar", s_ln, lambda e, s=xr: e.activation(ps_sq[:, :], s[:, 0:512], AF.Square))
    P.wait("vector", s_ln, sq1)
    r1 = P.ev("vector", s_ln, lambda e, l=lt: e.tensor_reduce(
        l[:, 0:1], ps_sq[:, :], axis=AX.X, op=ALU.add))
    P.wait("scalar", s_ln, r1)
    sq2 = P.ev("scalar", s_ln, lambda e, s=xr: e.activation(ps_sq[:, :], s[:, 512:1024], AF.Square))
    P.wait("vector", s_ln, sq2)
    r2 = P.ev("vector", s_ln, lambda e, l=lt: e.tensor_reduce(
        l[:, 4:5], ps_sq[:, :], axis=AX.X, op=ALU.add))
    nsv = P.ev("vector", s_ln, lambda e, s=xr, l=lt: e.tensor_reduce(
        l[:, 1:2], s[:, :], axis=AX.X, op=ALU.add, negate=True))
    P.wait("vector", s_ln, nsv)
    for fn in [
        lambda e, l=lt: e.tensor_add(l[:, 0:1], l[:, 0:1], l[:, 4:5]),
        lambda e, l=lt: e.tensor_scalar(l[:, 2:3], l[:, 1:2], 1.0 / D_, None, ALU.mult),
        lambda e, l=lt: e.tensor_mul(l[:, 3:4], l[:, 2:3], l[:, 2:3]),
        lambda e, l=lt: e.tensor_scalar(l[:, 4:5], l[:, 0:1], 1.0 / D_, None, ALU.mult),
        lambda e, l=lt: e.tensor_sub(l[:, 5:6], l[:, 4:5], l[:, 3:4]),
    ]:
        vx = P.ev("vector", s_ln, fn)
        P.wait("vector", s_ln, vx)
    vv = P.ev("vector", s_ln, lambda e, l=lt: e.tensor_scalar(
        l[:, 6:7], l[:, 5:6], 1e-5, None, ALU.add))
    P.wait("scalar", s_ln, vv)
    qv = P.ev("scalar", s_ln, lambda e, l=lt: e.activation(l[:, 7:8], l[:, 6:7], AF.Sqrt))
    P.wait("vector", s_ln, qv)
    rv = P.ev("vector", s_ln, lambda e, l=lt: e.reciprocal(l[:, 6:7], l[:, 7:8]))
    P.wait("vector", s_ln, rv)
    ov = P.ev("vector", s_ln, lambda e, s=xr, l=lt:
              e.tensor_scalar(s[:, :], s[:, :], l[:, 2:3], l[:, 6:7], ALU.add, ALU.mult))
    P.wait("gpsimd", s_ln, ov)
    sv = P.dma("gpsimd", y_dst_dram[tt * 128:(tt + 1) * 128, :], xr[:, :], s_st)
    xr_rot.release(xri, s_st, sv)



def build_prog_l0_attn():
    P = Prog("A")
    nc = P.nc

    x_in = nc.dram_tensor("x_in", [T, D], F32, kind="ExternalInput").ap()
    w_ins = [nc.dram_tensor(n, [D, D], F32, kind="ExternalInput").ap()
             for n in ("wq_in", "wk_in", "wv_in", "wo_in")]
    b_ins = [nc.dram_tensor(n, [1, D], F32, kind="ExternalInput").ap()
             for n in ("bq_in", "bk_in", "bv_in", "bo_in")]
    cos4_in = nc.dram_tensor("cos4_in", [128, T], F32, kind="ExternalInput").ap()
    sin4_in = nc.dram_tensor("sin4_in", [128, T], F32, kind="ExternalInput").ap()
    ident_in = nc.dram_tensor("ident_in", [128, 128], F32, kind="ExternalInput").ap()
    triu_in = nc.dram_tensor("triu_in", [128, 128], F32, kind="ExternalInput").ap()
    y_out = nc.dram_tensor("y_out", [T, D], F32, kind="ExternalOutput").ap()

    xf_d = nc.dram_tensor("xf_d", [ND, 128, T], BF16).ap()
    qd = nc.dram_tensor("qd", [4, 128, T], BF16).ap()

    ktb = [P.sbuf(f"ktb{c}", [128, T], BF16) for c in range(4)]
    vaug = [P.sbuf(f"vaug{t}", [128, 16 * 65], BF16) for t in range(NT)]
    cos4b = P.sbuf("cos4b", [128, T], BF16)
    sin4b = P.sbuf("sin4b", [128, T], BF16)
    ident_f = P.sbuf("ident_f", [128, 128], F32)
    ident_bf = P.sbuf("ident_bf", [128, 128], BF16)
    triu_bf = P.sbuf("triu_bf", [128, 128], BF16)
    ones_f32 = P.sbuf("ones_f32", [128, 128], F32)
    ones_bf = P.sbuf("ones_bf", [1, 512], BF16)
    P._ones_f32 = ones_f32
    biases = P.sbuf("biases", [1, 4 * D], BF16)
    wstage = P.sbuf("wstage", [128, 1024], F32)
    wbuf = Rot([P.sbuf(f"wbuf{i}", [128, 8192], BF16) for i in range(1)])
    x_stage = Rot([P.sbuf(f"x_stage{i}", [128, D], F32) for i in range(2)])
    xb_tile = Rot([P.sbuf(f"xb{i}", [128, D], BF16) for i in range(2)])
    xfsl = Rot([P.sbuf(f"xfsl{i}", [128, D], BF16) for i in range(2)])
    xsl = Rot([P.sbuf(f"xsl{i}", [128, 8 * 512], BF16) for i in range(1)])
    rtmp = P.sbuf("rtmp", [128, 512], F32)
    rtmp2 = P.sbuf("rtmp2", [128, 512], F32)
    qstg = Rot([P.sbuf(f"qstg{i}", [128, 512], BF16) for i in range(1)])
    qsl = Rot([P.sbuf(f"qsl{i}", [128, 4 * 512], BF16) for i in range(2)])
    yatt = [P.sbuf(f"yatt{c}", [128, 512], BF16) for c in range(ND)]
    ytmp = Rot([P.sbuf(f"ytmp{i}", [128, 512], BF16) for i in range(1)])
    y_blk = [P.sbuf(f"y_blk{i}", [128, D], BF16) for i in range(4)]
    lnt = [P.sbuf(f"lnt{i}", [128, 8], F32) for i in range(2)]
    bscr = P.sbuf("bscr", [128, 48], F32)
    negC = P.sbuf("negC", [128, 1], F32)

    ps_tr = Rot([P.psum("ps_tr0", [128, 128], BF16)])
    ps_proj = Rot([P.psum(f"ps_proj{i}", [128, 512], F32) for i in range(2)])
    ps_wo = Rot([P.psum("ps_wo0", [128, 512], F32)])
    ps_sq = P.psum("ps_sq", [128, 512], F32)

    s_ld = P.sem(); s_ld2 = P.sem(); s_cast = P.sem(); s_tr = P.sem(); s_ev = P.sem()
    s_mm = P.sem(); s_rope = P.sem(); s_exp = P.sem(); s_pv = P.sem(); s_evc = P.sem()
    s_msk = P.sem(); s_bnd = P.sem(); s_bnd2 = P.sem(); s_wo = P.sem(); s_ln = P.sem()
    s_st = P.sem(); s_misc = P.sem(); s_res = P.sem(); s_qd = P.sem(); s_ld3 = P.sem(); s_mv = P.sem()

    v = P.dma("sync", ident_f[:, :], ident_in, s_ld)
    P.wait("scalar", s_ld, v)
    cid = P.ev("scalar", s_cast, lambda e: e.copy(ident_bf[:, :], ident_f[:, :]))
    v = P.dma("sync", wstage[:, 0:128], triu_in, s_ld)
    P.wait("scalar", s_ld, v)
    ctr = P.ev("scalar", s_cast, lambda e: e.copy(triu_bf[:, :], wstage[:, 0:128]))
    P.wait("sync", s_cast, ctr)
    for bi_, b_in in enumerate(b_ins):
        v = P.dma("sync", wstage[0:1, 0:D], b_in, s_ld)
        P.wait("scalar", s_ld, v)
        cb = P.ev("scalar", s_cast, lambda e, i=bi_: e.copy(biases[:, i * D:(i + 1) * D], wstage[0:1, 0:D]))
        P.wait("sync", s_cast, cb)
    P.on("vector", lambda e: e.memset(ones_f32[:, :], 1.0))
    P.on("vector", lambda e: e.memset(ones_bf[:, :], 1.0))
    for ci, (c_in, c_sb) in enumerate([(cos4_in, cos4b), (sin4_in, sin4b)]):
        for c4 in range(4):
            v = P.dma("sync", wstage[:, 0:1024], c_in[:, c4 * 1024:(c4 + 1) * 1024], s_ld)
            P.wait("scalar", s_ld, v)
            cc = P.ev("scalar", s_cast, lambda e, d=c_sb, c=c4: e.copy(d[:, c * 1024:(c + 1) * 1024], wstage[:, 0:1024]))
            P.wait("sync", s_cast, cc)

    for tt in range(NT):
        st, si, g = x_stage.next()
        P.guard("gpsimd", g)
        v = P.dma("gpsimd", st[:, :], x_in[tt * 128:(tt + 1) * 128, :], s_ld2)
        xb, xi, g2 = xb_tile.next()
        P.guard("scalar", g2)
        P.wait("scalar", s_ld2, v)
        cv = P.ev("scalar", s_cast, lambda e, o=xb, i=st: e.copy(o[:, :], i[:, :]))
        x_stage.release(si, s_cast, cv)
        fs, fi, g4 = xfsl.next()
        P.guard("vector", g4)
        P.wait("tensor", s_cast, cv)
        last_ev = None
        for hc in range(ND):
            pb, pi, g3 = ps_tr.next()
            P.guard("tensor", g3)
            tv = P.ev("tensor", s_tr, lambda e, o=pb, i=xb, c=hc:
                      e.transpose(o[:, :], i[:, c * 128:(c + 1) * 128], ident_bf[:, :]))
            P.wait("vector", s_tr, tv)
            evv = P.ev("vector", s_ev, lambda e, o=fs, p=pb, c=hc:
                       e.tensor_copy(o[:, c * 128:(c + 1) * 128], p[:, :]))
            ps_tr.release(pi, s_ev, evv)
            last_ev = evv
        xb_tile.release(xi, s_tr, s_tr.n)
        P.wait("sync", s_ev, last_ev)
        dv = None
        for hc in range(ND):
            dv = P.dma("sync", xf_d[hc, :, tt * 128:(tt + 1) * 128], fs[:, hc * 128:(hc + 1) * 128], s_qd)
        xfsl.release(fi, s_qd, dv)
    xf_written = s_qd.n

    def load_weight(idx):
        wb, wi, g = wbuf.next()
        last = None
        for e8 in range(8):
            if g is not None:
                P.guard("sync", g)
            if last is not None:
                P.wait("sync", s_cast, last)
            v = P.dma("sync", wstage[:, :], w_ins[idx][e8 * 128:(e8 + 1) * 128, :], s_ld)
            P.wait("scalar", s_ld, v)
            last = P.ev("scalar", s_cast, lambda e, d=wb, s8=e8: e.copy(d[:, s8 * 1024:(s8 + 1) * 1024], wstage[:, :]))
        return wb, wi, last

    wq_b, wq_i, wq_r = load_weight(0)
    P.on("vector", lambda e: e.memset(bscr[:, 0:2], 0.0))
    zb = P.ev("vector", s_bnd2, lambda e: e.engine_nop())
    proj_mm_last = 0

    def xsl_load(tc, sl_rot):
        fs, fi, g = sl_rot.next()
        P.guard("sync", g)
        dv = None
        for hc in range(ND):
            dv = P.dma("sync", fs[:, hc * 512:(hc + 1) * 512],
                       xf_d[hc, :, tc * 512:(tc + 1) * 512], s_ld3)
        return fs, fi, dv

    P.wait("sync", s_qd, xf_written)

    for which in range(2):
        if which == 1:
            wb, wi_, wr = load_weight(1)
        else:
            wb, wi_, wr = wq_b, wq_i, wq_r
        P.wait("tensor", s_cast, wr)
        boff = which * D
        for tc in range(NQC):
            fs, fi, fv = xsl_load(tc, xsl)
            P.wait("tensor", s_ld3, fv)
            tcs = slice(tc * 512, (tc + 1) * 512)
            last_rope = None
            for p4 in range(4):
                pbs = []
                for par in range(2):
                    pb, pi, g = ps_proj.next()
                    P.guard("tensor", g)
                    col0 = 256 * p4 + 128 * par
                    for hc in range(ND):
                        P.mm(pb[:, :], wb[:, hc * 1024 + col0: hc * 1024 + col0 + 128],
                             fs[:, hc * 512:(hc + 1) * 512], start=(hc == 0), stop=False)
                    mv = P.mm(pb[:, :], biases[:, boff + col0: boff + col0 + 128],
                              ones_bf[:, :], start=False, stop=True, sem=s_mm)
                    pbs.append((pb, pi, mv))
                proj_mm_last = pbs[-1][2]
                P.wait("vector", s_mm, proj_mm_last)
                P.on("vector", lambda e, p=pbs[0][0], t=tcs: e.tensor_mul(rtmp[:, :], p[:, :], cos4b[:, t]))
                mulv = P.ev("vector", s_rope, lambda e, p=pbs[1][0], t=tcs:
                            e.tensor_mul(rtmp2[:, :], p[:, :], sin4b[:, t]))
                ps_proj.release(pbs[0][1], s_rope, mulv)
                ps_proj.release(pbs[1][1], s_rope, mulv)
                if which == 0:
                    qg, qi, gq = qstg.next()
                    P.guard("vector", gq)
                    sv = P.ev("vector", s_rope, lambda e, o=qg: e.tensor_sub(
                        o[:, :], rtmp[:, :], rtmp2[:, :]))
                    P.wait("vector", s_rope, sv)
                    rdv = P.ev("vector", s_bnd2, lambda e, o=qg: e.tensor_reduce(
                        bscr[:, 2:3], o[:, :], axis=AX.X, op=ALU.max, apply_absolute_value=True))
                    P.wait("vector", s_bnd2, rdv)
                    mxv = P.ev("vector", s_bnd2, lambda e: e.tensor_max(
                        bscr[:, 0:1], bscr[:, 0:1], bscr[:, 2:3]))
                    P.wait("sync", s_rope, sv)
                    dv = P.dma("sync", qd[p4, :, tcs], qg[:, :], s_qd)
                    qstg.release(qi, s_qd, dv)
                    last_rope = sv
                else:
                    sv = P.ev("vector", s_rope, lambda e, o=ktb[p4], t=tcs:
                              e.tensor_sub(o[:, t], rtmp[:, :], rtmp2[:, :]))
                    last_rope = sv
            xsl.release(fi, s_rope, last_rope)
        if which == 0:
            wbuf.release(wq_i, s_mm, proj_mm_last)
        else:
            wbuf.release(wi_, s_mm, proj_mm_last)
    qk_done = s_rope.n

    P.wait("vector", s_rope, qk_done)
    for c in range(4):
        rdv = P.ev("vector", s_bnd2, lambda e, a=ktb[c]: e.tensor_reduce(
            bscr[:, 2:3], a[:, :], axis=AX.X, op=ALU.max, apply_absolute_value=True))
        P.wait("vector", s_bnd2, rdv)
        mxv = P.ev("vector", s_bnd2, lambda e: e.tensor_max(
            bscr[:, 0:1], bscr[:, 0:1], bscr[:, 2:3]))
        P.wait("vector", s_bnd2, mxv)

    wb_v, wi_v, wr_v = load_weight(2)
    P.wait("tensor", s_cast, wr_v)
    v_mm_last = 0
    for tc in range(NQC):
        fs, fi, fv = xsl_load(tc, xsl)
        P.wait("tensor", s_ld3, fv)
        last_evac = None
        for j in range(4):
            tt = tc * 4 + j
            pbs = []
            for half in range(2):
                pb, pi, g = ps_proj.next()
                P.guard("tensor", g)
                for hc in range(ND):
                    P.mm(pb[:, :], fs[:, hc * 512 + j * 128: hc * 512 + j * 128 + 128],
                         wb_v[:, hc * 1024 + half * 512: hc * 1024 + (half + 1) * 512],
                         start=(hc == 0), stop=False)
                mv = P.mm(pb[:, :], ones_bf[:, 0:128],
                          biases[:, 2 * D + half * 512:2 * D + (half + 1) * 512],
                          start=False, stop=True, sem=s_mm)
                v_mm_last = mv
                pbs.append((pb, pi, mv))
            P.wait("scalar", s_mm, pbs[-1][2])
            dst = vaug[tt].ap().rearrange("p (h c) -> p h c", c=65)
            for half, (pb, pi, mv) in enumerate(pbs):
                vv = P.ev("scalar", s_misc, lambda e, d=dst, p=pb, hf=half:
                          e.copy(d[:, hf * 8:(hf + 1) * 8, 0:64],
                                 p[:, :].rearrange("p (h c) -> p h c", c=64)))
                ps_proj.release(pi, s_misc, vv)
                last_evac = vv
        xsl.release(fi, s_mm, v_mm_last)
    v_done = s_misc.n
    wbuf.release(wi_v, s_mm, v_mm_last)
    P.wait("vector", s_misc, v_done)
    for tt in range(NT):
        dst = vaug[tt].ap().rearrange("p (h c) -> p h c", c=65)
        P.on("vector", lambda e, d=dst: e.memset(d[:, :, 64:65], 1.0))
    vones = P.ev("vector", s_misc, lambda e: e.engine_nop())

    wb_o, wi_o, wr_o = load_weight(3)

    P.wait("vector", s_bnd2, s_bnd2.n)
    P.wait("tensor", s_mm, v_mm_last)
    ncv = emit_negC(P, bscr, negC, mult=4.0, clamp=60.0, s_chain=s_bnd,
                    s_extra=s_bnd2, bc_ps=ps_proj.bufs[0], tr_ps=ps_wo.bufs[0], ident_f=ident_f)
    P.wait("scalar", s_bnd, ncv)
    P.wait("tensor", s_misc, vones)
    P.wait("scalar", s_misc, vones)

    ctx = {"xr_rot": x_stage, "lnt": lnt, "ps_sq": ps_sq,
           "s_ld2": s_ld2, "s_ln": s_ln, "s_st": s_st}

    def get_q(qc):
        fs, fi, g = qsl.next()
        P.guard("sync", g)
        dv = None
        for c in range(4):
            dv = P.dma("sync", fs[:, c * 512:(c + 1) * 512], qd[c, :, qc * 512:(qc + 1) * 512], s_ld3)
        chunks = [fs[:, c * 512:(c + 1) * 512] for c in range(4)]
        qsl._pending = fi
        return chunks, (s_ld3, dv)

    def after_qc(qc, evc_val, mv_val):
        qsl.release(qsl._pending, s_mm, s_mm.n)
        P.wait("tensor", s_evc, evc_val)
        P.wait("tensor", s_mv, mv_val)
        P.wait("tensor", s_cast, wr_o)
        wo_last = 0
        for oc in range(ND):
            pb, pi, g = ps_wo.next()
            P.guard("tensor", g)
            for dc in range(ND):
                P.mm(pb[:, :], wb_o[:, dc * 1024 + oc * 128: dc * 1024 + oc * 128 + 128],
                     yatt[dc][:, :], start=(dc == 0), stop=False)
            mv = P.mm(pb[:, :], biases[:, 3 * D + oc * 128: 3 * D + oc * 128 + 128],
                      ones_bf[:, :], start=False, stop=True, sem=s_wo)
            wo_last = mv
            P.wait("scalar", s_wo, mv)
            yb, yi, g2 = ytmp.next()
            P.guard("scalar", g2)
            cv = P.ev("scalar", s_wo, lambda e, y=yb, p=pb: e.copy(y[:, :], p[:, :]))
            ps_wo.release(pi, s_wo, cv)
            P.wait("tensor", s_wo, cv)
            lastt = None
            for j in range(4):
                pt_, pti, g3 = ps_tr.next()
                P.guard("tensor", g3)
                tv = P.ev("tensor", s_tr, lambda e, o=pt_, i=yb, j=j:
                          e.transpose(o[:, :], i[:, j * 128:(j + 1) * 128], ident_bf[:, :]))
                P.wait("vector", s_tr, tv)
                evv = P.ev("vector", s_res, lambda e, o=y_blk[j], p=pt_, c=oc:
                           e.tensor_copy(o[:, c * 128:(c + 1) * 128], p[:, :]))
                ps_tr.release(pti, s_res, evv)
                lastt = tv
            ytmp.release(yi, s_tr, lastt)
        yblk_done = s_res.n
        P.wait("vector", s_res, yblk_done)
        for j in range(4):
            emit_postln(P, ctx, qc * 4 + j, y_blk[j][:, :], x_in, y_out)
        return (s_wo, wo_last)

    emit_attention(P, dict(
        tag="a", ktb=ktb, vaug=vaug, yatt=yatt, nheads=16, causal=True, dh=32,
        negC=negC, scale=0.125, triu_bf=triu_bf, get_q=get_q, after_qc=after_qc,
        s_mm=s_mm, s_exp=s_exp, s_pv=s_pv, s_evc=s_evc, s_msk=s_msk, s_mv=s_mv))

    P.wait("gpsimd", s_st, s_st.n)
    return P


def _prep_l0_attn_inputs(x_full, Wq, bq, Wk, bk, Wv, bv, Wo, bo):
    consts = host_consts()
    perm = perm_rope_cols()
    base = {
        "wq_in": np.ascontiguousarray(Wq.T[:, perm]),
        "wk_in": np.ascontiguousarray(Wk.T[:, perm]),
        "wv_in": np.ascontiguousarray(Wv.T),
        "wo_in": np.ascontiguousarray(Wo.T),
        "bq_in": np.ascontiguousarray(bq[perm])[None, :],
        "bk_in": np.ascontiguousarray(bk[perm])[None, :],
        "bv_in": np.ascontiguousarray(bv)[None, :],
        "bo_in": np.ascontiguousarray(bo)[None, :],
        "cos4_in": consts["cos4"], "sin4_in": consts["sin4"],
        "ident_in": consts["ident"], "triu_in": consts["triu"],
    }
    in_maps = []
    for c in range(NCORES):
        m = dict(base)
        m["x_in"] = np.ascontiguousarray(x_full[c])
        in_maps.append(m)
    return in_maps


def run_l0_attn(x_full, Wq, bq, Wk, bk, Wv, bv, Wo, bo):
    in_maps = _prep_l0_attn_inputs(x_full, Wq, bq, Wk, bk, Wv, bv, Wo, bo)
    P = build_prog_l0_attn()
    nc = P.finalize()
    res = run_bass_kernel_spmd(nc, in_maps, core_ids=list(range(NCORES)))
    return np.stack([r["y_out"] for r in res.results], axis=0)



_PROGS = {}


def _get_prog(name, builder):
    if name not in _PROGS:
        _PROGS[name] = builder().finalize()
    return _PROGS[name]


def kernel(**inputs):
    x = np.ascontiguousarray(np.asarray(inputs["L0_attn_x"], dtype=np.float32))
    g = lambda k: np.asarray(inputs[k], dtype=np.float32)
    cores = list(range(NCORES))

    m = _prep_l0_attn_inputs(x, g("L0_attn_Wq"), g("L0_attn_bq"), g("L0_attn_Wk"), g("L0_attn_bk"),
                             g("L0_attn_Wv"), g("L0_attn_bv"), g("L0_attn_Wo"), g("L0_attn_bo"))
    nc = _get_prog("A", build_prog_l0_attn)
    res = run_bass_kernel_spmd(nc, m, core_ids=cores)
    x = np.stack([r["y_out"] for r in res.results], axis=0)

    m = _prep_l0_ffn_inputs(x, g("L0_ffn_Wg"), g("L0_ffn_W1"), g("L0_ffn_W2"))
    nc = _get_prog("B", build_prog_l0_ffn)
    res = run_bass_kernel_spmd(nc, m, core_ids=cores)
    x = np.stack([r["y_out"] for r in res.results], axis=0)

    m = _prep_l1_attn_inputs(x, g("L1_attn_w_qkv"), g("L1_attn_w_out"), g("L1_attn_b_out"),
                             g("L1_attn_ln_g"), g("L1_attn_ln_b"), g("L1_attn_w1"), g("L1_attn_b1"),
                             g("L1_attn_w2"), g("L1_attn_b2"))
    nc = _get_prog("C", build_prog_l1_attn)
    res = run_bass_kernel_spmd(nc, m, core_ids=cores)
    x = np.stack([r["y_out"] for r in res.results], axis=0)

    names = ["slot_init", "w1", "b1", "w2", "b2", "tau", "Wih", "Whh", "bih", "bhh",
             "wq", "bq", "wk", "bk", "wv", "bv", "ln_g", "ln_b", "wvp", "bvp", "wo", "bo"]
    m = _prep_l1_ffn_inputs(x, *[g("L1_ffn_" + n) for n in names])
    nc = _get_prog("D", build_prog_l1_ffn)
    res = run_bass_kernel_spmd(nc, m, core_ids=cores)
    return np.stack([r["y_out"] for r in res.results], axis=0)



E_EXP = 8
CAP = 512
HFF = 4096


def build_prog_l0_ffn():
    P = Prog("B")
    nc = P.nc

    x_in = nc.dram_tensor("x_in", [T, D], F32, kind="ExternalInput").ap()
    wg_in = nc.dram_tensor("wg_in", [D, E_EXP], F32, kind="ExternalInput").ap()
    w1_in = nc.dram_tensor("w1_in", [E_EXP, D, HFF], F32, kind="ExternalInput").ap()
    w2_in = nc.dram_tensor("w2_in", [E_EXP, HFF, D], F32, kind="ExternalInput").ap()
    ident_in = nc.dram_tensor("ident_in", [128, 128], F32, kind="ExternalInput").ap()
    triu_in = nc.dram_tensor("triu_in", [128, 128], F32, kind="ExternalInput").ap()
    iota512_in = nc.dram_tensor("iota512_in", [128, 512], F32, kind="ExternalInput").ap()
    iotap_in = nc.dram_tensor("iotap_in", [128, 1], F32, kind="ExternalInput").ap()
    eoff_in = nc.dram_tensor("eoff_in", [128, 8], F32, kind="ExternalInput").ap()
    y_out = nc.dram_tensor("y_out", [T, D], F32, kind="ExternalOutput").ap()
    ef_d = nc.dram_tensor("ef_d", [ND, 128, T], BF16).ap()

    ident_f = P.sbuf("ident_f", [128, 128], F32)
    ident_bf = P.sbuf("ident_bf", [128, 128], BF16)
    tri_f = P.sbuf("tri_f", [128, 128], F32)
    ones_f32 = P.sbuf("ones_f32", [128, 128], F32)
    P._ones_f32 = ones_f32
    iota512 = P.sbuf("iota512", [128, 512], F32)
    iotap = P.sbuf("iotap", [128, 1], F32)
    eoffc = P.sbuf("eoffc", [128, 8], F32)
    gcola = P.sbuf("gcola", [128, NT], F32)
    gcolb = P.sbuf("gcolb", [128, NT], F32)
    wg_f = P.sbuf("wg_f", [128, 64], F32)
    wg_b = P.sbuf("wg_b", [128, 64], BF16)
    x_stage = Rot([P.sbuf(f"x_stage{i}", [128, D], F32) for i in range(2)])
    lnt = [P.sbuf(f"lnt{i}", [128, 8], F32) for i in range(2)]
    offs = P.sbuf("offs", [1, 8], F32)
    lsc = P.sbuf("lsc", [128, 24], F32)
    colrow = P.sbuf("colrow", [1, 128], F32)

    s_ld = P.sem(); s_ld2 = P.sem(); s_ld3 = P.sem(); s_cast = P.sem(); s_tr = P.sem()
    s_ev = P.sem(); s_mm = P.sem(); s_r = P.sem(); s_exp = P.sem(); s_ln = P.sem()
    s_st = P.sem(); s_misc = P.sem(); s_ef = P.sem(); s_h = P.sem(); s_eo = P.sem()
    s_dt = P.sem(); s_cb = P.sem()

    P.dma("sync", tri_f[:, :], triu_in, s_ld)
    P.dma("sync", iota512[:, :], iota512_in, s_ld)
    P.dma("sync", iotap[:, :], iotap_in, s_ld)
    P.dma("sync", eoffc[:, :], eoff_in, s_ld)
    wg3 = wg_in.rearrange("(c p) e -> p c e", p=128)
    P.dma("sync", wg_f[:, :].rearrange("p (c e) -> p c e", e=8), wg3, s_ld)
    idv = P.dma("sync", ident_f[:, :], ident_in, s_ld)
    P.wait("scalar", s_ld, idv)
    cid = P.ev("scalar", s_cast, lambda e: e.copy(ident_bf[:, :], ident_f[:, :]))
    cwg = P.ev("scalar", s_cast, lambda e: e.copy(wg_b[:, :], wg_f[:, :]))
    P.on("vector", lambda e: e.memset(ones_f32[:, :], 1.0))
    P.on("vector", lambda e: e.memset(offs[:, :], 0.0))
    zo = P.ev("vector", s_misc, lambda e: e.engine_nop())

    x_tok = [P.sbuf(f"x_tok{t}", [128, D], BF16) for t in range(NT)]
    esR = contextlib.ExitStack()
    xtr = Rot([P.sbuf(f"xtr{i}", [128, D], F32, es=esR) for i in range(2)])
    e_sb = P.sbuf("e_sb", [128, 8], F32, es=esR)
    mask_sb = P.sbuf("mask_sb", [128, 8], F32, es=esR)
    ps_tr = Rot([P.psum("ps_tr0", [128, 128], F32, es=esR)])
    ps_log = P.psum("ps_log", [128, 8], F32, es=esR)
    ps_cs = P.psum("ps_cs", [128, 8], F32, es=esR)
    ps_tot = P.psum("ps_tot", [1, 8], F32, es=esR)

    P.wait("tensor", s_cast, cwg)
    P.wait("vector", s_misc, zo)
    for tt in range(NT):
        st_, si, g = x_stage.next()
        P.guard("gpsimd", g)
        v = P.dma("gpsimd", st_[:, :], x_in[tt * 128:(tt + 1) * 128, :], s_ld2)
        P.wait("scalar", s_ld2, v)
        cv = P.ev("scalar", s_cast, lambda e, o=x_tok[tt], i=st_: e.copy(o[:, :], i[:, :]))
        xt, xi, g2 = xtr.next()
        P.wait("tensor", s_ld2, v)
        lastev = None
        for hc in range(ND):
            pb, pi, g3 = ps_tr.next()
            P.guard("tensor", g3)
            tv = P.ev("tensor", s_tr, lambda e, o=pb, i=st_, c=hc:
                      e.transpose(o[:, :], i[:, c * 128:(c + 1) * 128], ident_f[:, :]))
            P.guard("vector", g2)
            P.wait("vector", s_tr, tv)
            lastev = P.ev("vector", s_ev, lambda e, o=xt, p=pb, c=hc:
                          e.tensor_copy(o[:, c * 128:(c + 1) * 128], p[:, :]))
            ps_tr.release(pi, s_ev, lastev)
        x_stage.release(si, s_tr, s_tr.n)
        P.wait("tensor", s_ev, lastev)
        P.wait("tensor", s_r, s_r.n)
        for hc in range(ND):
            P.mm(ps_log[:, :], xt[:, hc * 128:(hc + 1) * 128], wg_f[:, hc * 8:(hc + 1) * 8],
                 start=(hc == 0), stop=(hc == ND - 1), sem=(s_mm if hc == ND - 1 else None))
        xtr.release(xi, s_mm, s_mm.n)
        P.wait("vector", s_mm, s_mm.n)
        lv = P.ev("vector", s_r, lambda e: e.reduce_max(lsc[:, 0:1], ps_log[:, :], axis=AX.X))
        P.wait("vector", s_r, lv)
        lv = P.ev("vector", s_r, lambda e: e.tensor_scalar(lsc[:, 1:2], lsc[:, 0:1], -1.0, None, ALU.mult))
        P.wait("scalar", s_r, lv)
        sv = P.ev("scalar", s_exp, lambda e: e.activation(e_sb[:, :], ps_log[:, :], AF.Exp, bias=lsc[:, 1:2]))
        P.wait("vector", s_exp, sv)
        lv = P.ev("vector", s_r, lambda e: e.tensor_reduce(lsc[:, 2:3], e_sb[:, :], axis=AX.X, op=ALU.add))
        P.wait("vector", s_r, lv)
        lv = P.ev("vector", s_r, lambda e, t=tt: e.reciprocal(gcolb[:, t:t + 1], lsc[:, 2:3]))
        P.wait("vector", s_r, lv)
        mv_ = P.ev("vector", s_r, lambda e: e.tensor_scalar(mask_sb[:, :], ps_log[:, :], lsc[:, 0:1], None, ALU.is_ge))
        P.wait("tensor", s_r, mv_)
        P.mm(ps_cs[:, :], tri_f[:, :], mask_sb[:, :], start=True, stop=False)
        c1 = P.mm(ps_cs[:, :], ones_f32[0:1, 0:128], offs[:, :], start=False, stop=True, sem=s_mm)
        c2 = P.mm(ps_tot[:, :], ones_f32[:, 0:1], mask_sb[:, :], start=True, stop=True, sem=s_mm)
        P.wait("vector", s_mm, c2)
        uv = P.ev("vector", s_r, lambda e: e.tensor_add(offs[:, :], offs[:, :], ps_tot[:, :]))
        chain = [
            lambda e: e.tensor_add(lsc[:, 8:16], ps_cs[:, :], eoffc[:, :]),
            lambda e: e.tensor_mul(lsc[:, 8:16], lsc[:, 8:16], mask_sb[:, :]),
            lambda e: e.tensor_reduce(lsc[:, 3:4], lsc[:, 8:16], axis=AX.X, op=ALU.add),
            lambda e: e.tensor_mul(lsc[:, 16:24], ps_cs[:, :], mask_sb[:, :]),
            lambda e: e.tensor_reduce(lsc[:, 4:5], lsc[:, 16:24], axis=AX.X, op=ALU.add),
            lambda e: e.tensor_scalar(lsc[:, 5:6], lsc[:, 4:5], 512.0, None, ALU.is_le),
            lambda e: e.tensor_scalar(lsc[:, 6:7], lsc[:, 3:4], -1.0, None, ALU.add),
            lambda e: e.tensor_mul(lsc[:, 6:7], lsc[:, 6:7], lsc[:, 5:6]),
            lambda e: e.tensor_scalar(lsc[:, 7:8], lsc[:, 5:6], -1.0, None, ALU.add),
            lambda e, t=tt: e.tensor_add(gcola[:, t:t + 1], lsc[:, 6:7], lsc[:, 7:8]),
        ]
        for fn in chain:
            lv = P.ev("vector", s_r, fn)
            P.wait("vector", s_r, lv)
    routing_done = s_r.n
    esR.close()

    esD = contextlib.ExitStack()
    dslice = Rot([P.sbuf(f"dsl{i}", [128, 512], BF16, es=esD) for i in range(2)])
    efsl = P.sbuf("efsl", [128, ND * 512], BF16, es=esD)
    ps_e = [P.psum(f"ps_e{i}", [128, 512], F32, es=esD) for i in range(ND)]
    P.wait("vector", s_r, routing_done)
    for ec in range(8):
        for tt in range(NT):
            db, di, g = dslice.next()
            P.guard("vector", g)
            bv = P.ev("vector", s_dt, lambda e, d=db, t=tt, ec=ec: e.tensor_scalar(
                d[:, :], iota512[:, :], float(ec * 512), gcola[:, t:t + 1],
                ALU.add, ALU.is_equal))
            P.wait("tensor", s_dt, bv)
            if tt == 0:
                P.wait("tensor", s_ef, s_ef.n)
            for dc in range(ND):
                P.mm(ps_e[dc][:, :], x_tok[tt][:, dc * 128:(dc + 1) * 128], db[:, :],
                     start=(tt == 0), stop=(tt == NT - 1),
                     sem=(s_mm if dc == ND - 1 else None))
            dslice.release(di, s_mm, s_mm.n)
        P.wait("scalar", s_mm, s_mm.n)
        evl = None
        for dc in range(ND):
            evl = P.ev("scalar", s_ef, lambda e, d=dc: e.copy(
                efsl[:, d * 512:(d + 1) * 512], ps_e[d][:, :]))
        P.wait("sync", s_ef, evl)
        for dc in range(ND):
            P.dma("sync", ef_d[dc, :, ec * 512:(ec + 1) * 512], efsl[:, dc * 512:(dc + 1) * 512], s_ld3)
    disp_done = s_ld3.n
    esD.close()

    eo = [P.sbuf(f"eo{i}", [128, D], BF16) for i in range(NT)]
    esE = contextlib.ExitStack()
    hT = [P.sbuf(f"hT{i}", [128, 512], BF16, es=esE) for i in range(32)]
    eslice = P.sbuf("eslice", [128, ND * 512], BF16, es=esE)
    wstage = Rot([P.sbuf(f"wstage{i}", [128, 512], F32, es=esE) for i in range(2)])
    w1t = Rot([P.sbuf(f"w1t{i}", [128, 128], BF16, es=esE) for i in range(3)])
    w2t = Rot([P.sbuf(f"w2t{i}", [128, 512], BF16, es=esE) for i in range(2)])
    ps_h = Rot([P.psum(f"ps_h{i}", [128, 512], F32, es=esE) for i in range(2)])
    ps_o4 = [P.psum(f"ps_o4{i}", [128, 512], F32, es=esE) for i in range(4)]

    P.wait("sync", s_ld3, disp_done)
    for ex in range(E_EXP):
        lv = None
        for dc in range(ND):
            lv = P.dma("sync", eslice[:, dc * 512:(dc + 1) * 512],
                       ef_d[dc, :, ex * 512:(ex + 1) * 512], s_ld3)
        P.wait("tensor", s_ld3, lv)
        w2_guard = s_mm.n
        P.wait("scalar", s_mm, w2_guard)
        for Hc in range(32):
            pb, pi, g = ps_h.next()
            P.guard("tensor", g)
            for dc in range(ND):
                wt, wi, gw = w1t.next()
                sb_, sbi, gs = wstage.next()
                P.guard("sync", gs)
                dv = P.dma("sync", sb_[:, 0:128],
                           w1_in[ex, dc * 128:(dc + 1) * 128, Hc * 128:(Hc + 1) * 128], s_ld)
                P.guard("scalar", gw)
                P.wait("scalar", s_ld, dv)
                cv = P.ev("scalar", s_cast, lambda e, o=wt, i=sb_: e.copy(o[:, :], i[:, 0:128]))
                wstage.release(sbi, s_cast, cv)
                P.wait("tensor", s_cast, cv)
                P.mm(pb[:, :], wt[:, :], eslice[:, dc * 512:(dc + 1) * 512],
                     start=(dc == 0), stop=(dc == ND - 1), sem=s_mm)
                w1t.release(wi, s_mm, s_mm.n)
            P.wait("scalar", s_mm, s_mm.n)
            hv = P.ev("scalar", s_h, lambda e, o=hT[Hc], p=pb: e.activation(
                o[:, :], p[:, :], AF.Gelu_apprx_tanh))
            ps_h.release(pi, s_h, hv)
        h_done = s_h.n
        P.wait("tensor", s_h, h_done)
        for grp in range(2):
            for Hc in range(32):
                wt, wi, gw = w2t.next()
                sb_, sbi, gs = wstage.next()
                P.guard("sync", gs)
                dv = P.dma("sync", sb_[:, :],
                           w2_in[ex, Hc * 128:(Hc + 1) * 128, grp * 512:(grp + 1) * 512], s_ld)
                P.guard("scalar", gw)
                P.wait("scalar", s_ld, dv)
                cv = P.ev("scalar", s_cast, lambda e, o=wt, i=sb_: e.copy(o[:, :], i[:, :]))
                wstage.release(sbi, s_cast, cv)
                P.wait("tensor", s_cast, cv)
                if Hc == 0:
                    P.wait("tensor", s_eo, s_eo.n)
                for cg in range(4):
                    P.mm(ps_o4[cg][:, :], hT[Hc][:, cg * 128:(cg + 1) * 128], wt[:, :],
                         start=(Hc == 0), stop=(Hc == 31),
                         sem=(s_mm if cg == 3 else None))
                w2t.release(wi, s_mm, s_mm.n)
            P.wait("scalar", s_mm, s_mm.n)
            for cg in range(4):
                P.ev("scalar", s_eo, lambda e, o=eo[ex * 4 + cg], p=ps_o4[cg], gp=grp:
                     e.copy(o[:, gp * 512:(gp + 1) * 512], p[:, :]))
    eo_done = s_eo.n
    esE.close()

    esC = contextlib.ExitStack()
    dtile = Rot([P.sbuf(f"dtile{i}", [128, 128], BF16, es=esC) for i in range(2)])
    ytile = Rot([P.sbuf(f"ytile{i}", [128, D], BF16, es=esC) for i in range(2)])
    ps_ct = P.psum("ps_ct", [1, 128], F32, es=esC)
    ps_cr = P.psum("ps_cr", [128, 128], F32, es=esC)
    ps_cmb = Rot([P.psum(f"ps_cmb{i}", [128, 512], F32, es=esC) for i in range(4)])
    ps_sq = P.psum("ps_sq", [128, 512], F32, es=esC)
    ctx = {"xr_rot": x_stage, "lnt": lnt, "ps_sq": ps_sq,
           "s_ld2": s_ld2, "s_ln": s_ln, "s_st": s_st}
    P.wait("scalar", s_eo, eo_done)
    P.wait("tensor", s_eo, eo_done)
    for tt in range(NT):
        P.wait("tensor", s_cb, s_cb.n)
        tv = P.ev("tensor", s_cb, lambda e, t=tt: e.transpose(
            ps_ct[:, :], gcola[:, t:t + 1], ident_f[:, :]))
        P.wait("vector", s_cb, tv)
        cv = P.ev("vector", s_cb, lambda e: e.tensor_copy(colrow[:, :], ps_ct[:, :]))
        P.wait("tensor", s_cb, cv)
        bv = P.ev("tensor", s_cb, lambda e: e.matmul(
            ps_cr[:, :], ones_f32[0:1, 0:128], colrow[:, :], start=True, stop=True))
        P.wait("vector", s_cb, bv)
        pcm, pcmi, gcm = ps_cmb.next()
        pcm2, pcmi2, gcm2 = ps_cmb.next()
        P.guard("tensor", gcm)
        P.guard("tensor", gcm2)
        for ecT in range(32):
            db, di, g = dtile.next()
            P.guard("vector", g)
            dv = P.ev("vector", s_dt, lambda e, d=db, et=ecT: e.tensor_scalar(
                d[:, :], ps_cr[:, :], float(-et * 128), iotap[:, :], ALU.add, ALU.is_equal))
            P.wait("tensor", s_dt, dv)
            P.mm(pcm[:, :], db[:, :], eo[ecT][:, 0:512], start=(ecT == 0), stop=(ecT == 31))
            P.mm(pcm2[:, :], db[:, :], eo[ecT][:, 512:1024], start=(ecT == 0), stop=(ecT == 31),
                 sem=s_mm)
            dtile.release(di, s_mm, s_mm.n)
        P.wait("vector", s_mm, s_mm.n)
        yt, yti, gy = ytile.next()
        P.guard("vector", gy)
        g1 = P.ev("vector", s_cb, lambda e, y=yt, p=pcm, t=tt: e.tensor_scalar(
            y[:, 0:512], p[:, :], gcolb[:, t:t + 1], None, ALU.mult))
        g2 = P.ev("vector", s_cb, lambda e, y=yt, p=pcm2, t=tt: e.tensor_scalar(
            y[:, 512:1024], p[:, :], gcolb[:, t:t + 1], None, ALU.mult))
        ps_cmb.release(pcmi, s_cb, g1)
        ps_cmb.release(pcmi2, s_cb, g2)
        P.wait("vector", s_cb, g2)
        emit_postln(P, ctx, tt, yt[:, :], x_in, y_out)
        ytile.release(yti, s_ln, s_ln.n)
    esC.close()
    P.wait("gpsimd", s_st, s_st.n)
    return P


def _prep_l0_ffn_inputs(x_full, Wg, W1, W2):
    consts = host_consts()
    base = {
        "wg_in": np.ascontiguousarray(Wg),
        "w1_in": np.ascontiguousarray(W1),
        "w2_in": np.ascontiguousarray(W2),
        "ident_in": consts["ident"], "triu_in": consts["triu"],
        "iota512_in": np.tile(np.arange(512, dtype=np.float32)[None, :], (128, 1)),
        "iotap_in": np.arange(128, dtype=np.float32)[:, None],
        "eoff_in": np.tile((np.arange(8, dtype=np.float32) * 512)[None, :], (128, 1)),
    }
    in_maps = []
    for c in range(NCORES):
        m = dict(base)
        m["x_in"] = np.ascontiguousarray(x_full[c])
        in_maps.append(m)
    return in_maps


def run_l0_ffn(x_full, Wg, W1, W2):
    in_maps = _prep_l0_ffn_inputs(x_full, Wg, W1, W2)
    P = build_prog_l0_ffn()
    nc = P.finalize()
    res = run_bass_kernel_spmd(nc, in_maps, core_ids=list(range(NCORES)))
    return np.stack([r["y_out"] for r in res.results], axis=0)



def build_prog_l1_attn():
    P = Prog("C")
    nc = P.nc

    x_in = nc.dram_tensor("x_in", [T, D], F32, kind="ExternalInput").ap()
    wqkv_in = nc.dram_tensor("wqkv_in", [D, 3 * D], F32, kind="ExternalInput").ap()
    wout_in = nc.dram_tensor("wout_in", [D, D], F32, kind="ExternalInput").ap()
    bout_in = nc.dram_tensor("bout_in", [1, D], F32, kind="ExternalInput").ap()
    lng_in = nc.dram_tensor("lng_in", [64, 1], F32, kind="ExternalInput").ap()
    lnb_in = nc.dram_tensor("lnb_in", [64, 1], F32, kind="ExternalInput").ap()
    w1s_in = nc.dram_tensor("w1s_in", [64, 16], F32, kind="ExternalInput").ap()
    b1s_in = nc.dram_tensor("b1s_in", [1, 16], F32, kind="ExternalInput").ap()
    w2s_in = nc.dram_tensor("w2s_in", [16, 1], F32, kind="ExternalInput").ap()
    b2s_in = nc.dram_tensor("b2s_in", [1, 1], F32, kind="ExternalInput").ap()
    sel_in = nc.dram_tensor("sel_in", [128, 64], F32, kind="ExternalInput").ap()
    ident_in = nc.dram_tensor("ident_in", [128, 128], F32, kind="ExternalInput").ap()
    y_out = nc.dram_tensor("y_out", [T, D], F32, kind="ExternalOutput").ap()
    xf_d = nc.dram_tensor("xf_d", [ND, 128, T], BF16).ap()
    qd = nc.dram_tensor("qd", [ND, 128, T], BF16).ap()
    qm_d = nc.dram_tensor("qm_d", [64, T], F32).ap()

    ktb = [P.sbuf(f"ktb{c}", [128, T], BF16) for c in range(ND)]
    vaug = [P.sbuf(f"vaug{t}", [128, 16 * 65], BF16) for t in range(NT)]
    grep = P.sbuf("grep", [128, T], BF16)
    ident_f = P.sbuf("ident_f", [128, 128], F32)
    ident_bf = P.sbuf("ident_bf", [128, 128], BF16)
    ones_f32 = P.sbuf("ones_f32", [128, 128], F32)
    P._ones_f32 = ones_f32
    biases = P.sbuf("biases", [1, D], BF16)
    selb = P.sbuf("selb", [128, 64], BF16)
    somap = P.sbuf("somap", [64, 2], F32)
    w1sb = P.sbuf("w1sb", [64, 16], BF16)
    b1sb = P.sbuf("b1sb", [1, 16], BF16)
    w2sb = P.sbuf("w2sb", [16, 1], BF16)
    b2sb = P.sbuf("b2sb", [1, 1], BF16)
    ones_bf = P.sbuf("ones_bf", [1, 512], BF16)
    scol = P.sbuf("scol", [128, 32], F32)
    bsc = P.sbuf("bsc", [128, 8], F32)
    cmp32 = P.sbuf("cmp32", [128, 32], F32)
    sig32 = P.sbuf("sig32", [128, 32], F32)
    bscr = P.sbuf("bscr", [128, 64], F32)
    negC = P.sbuf("negC", [128, 1], F32)
    wstage = P.sbuf("wstage", [128, 512], F32)
    wbuf = Rot([P.sbuf(f"wbuf{i}", [128, 8192], BF16) for i in range(1)])
    x_stage = Rot([P.sbuf(f"x_stage{i}", [128, D], F32) for i in range(1)])
    xb_tile = Rot([P.sbuf(f"xb{i}", [128, D], BF16) for i in range(1)])
    xfsl = Rot([P.sbuf(f"xfsl{i}", [128, D], BF16) for i in range(1)])
    xsl = Rot([P.sbuf(f"xsl{i}", [128, 8 * 512], BF16) for i in range(1)])
    qsb = Rot([P.sbuf(f"qsb{i}", [128, 512], BF16) for i in range(1)])
    yatt = [P.sbuf(f"yatt{c}", [128, 512], BF16) for c in range(ND)]
    y_blk = [P.sbuf(f"y_blk{i}", [128, D], BF16) for i in range(4)]
    ytmp = qsb
    lnt = [P.sbuf(f"lnt{i}", [128, 8], F32) for i in range(2)]

    ps_tr = Rot([P.psum("ps_tr0", [128, 128], BF16)])
    ps_proj = Rot([P.psum(f"ps_proj{i}", [128, 512], F32) for i in range(2)])
    ps_wo = Rot([P.psum("ps_wo0", [128, 512], F32)])

    s_ld = P.sem(); s_ld2 = P.sem(); s_ld3 = P.sem(); s_cast = P.sem(); s_tr = P.sem()
    s_ev = P.sem(); s_mm = P.sem(); s_exp = P.sem(); s_pv = P.sem(); s_evc = P.sem()
    s_msk = P.sem(); s_bnd = P.sem(); s_bnd2 = P.sem(); s_wo = P.sem(); s_ln = P.sem()
    s_st = P.sem(); s_misc = P.sem(); s_res = P.sem(); s_qd = P.sem(); s_mv = P.sem()
    s_g = P.sem()

    esS = contextlib.ExitStack()
    ps_sm = P.psum("ps_sm", [64, 512], F32, es=esS)
    ps_sm2 = P.psum("ps_sm2", [64, 512], F32, es=esS)
    ps_ct = P.psum("ps_ct", [128, 4], F32, es=esS)
    ps_c1 = P.psum("ps_c1", [64, 128], F32, es=esS)
    qmsl = P.sbuf("qmsl", [64, 512], F32, es=esS)
    srow = P.sbuf("srow", [1, 512], F32, es=esS)
    growsl = srow

    idv = P.dma("sync", ident_f[:, :], ident_in, s_ld)
    P.wait("scalar", s_ld, idv)
    P.ev("scalar", s_cast, lambda e: e.copy(ident_bf[:, :], ident_f[:, :]))
    for bh in range(2):
        v = P.dma("sync", wstage[0:1, 0:512], bout_in[:, bh * 512:(bh + 1) * 512], s_ld)
        P.wait("scalar", s_ld, v)
        cb = P.ev("scalar", s_cast, lambda e, bh=bh: e.copy(biases[:, bh * 512:(bh + 1) * 512], wstage[0:1, 0:512]))
        P.wait("sync", s_cast, cb)
    P.dma("sync", somap[:, 0:1], lng_in, s_ld)
    P.dma("sync", somap[:, 1:2], lnb_in, s_ld)
    P.dma("sync", wstage[0:64, 0:16], w1s_in, s_ld)
    P.dma("sync", wstage[0:16, 16:17], w2s_in, s_ld)
    P.dma("sync", wstage[0:1, 17:18], b2s_in, s_ld)
    P.dma("sync", wstage[0:1, 32:48], b1s_in, s_ld)
    v = P.dma("sync", wstage[0:128, 64:128], sel_in, s_ld)
    P.wait("scalar", s_ld, v)
    P.on("scalar", lambda e: e.copy(w1sb[:, :], wstage[0:64, 0:16]))
    P.on("scalar", lambda e: e.copy(w2sb[:, :], wstage[0:16, 16:17]))
    P.on("scalar", lambda e: e.copy(b2sb[:, :], wstage[0:1, 17:18]))
    P.on("scalar", lambda e: e.copy(b1sb[:, :], wstage[0:1, 32:48]))
    cs = P.ev("scalar", s_cast, lambda e: e.copy(selb[:, :], wstage[0:128, 64:128]))
    P.wait("sync", s_cast, cs)
    P.on("vector", lambda e: e.memset(ones_f32[:, :], 1.0))
    P.on("vector", lambda e: e.memset(ones_bf[:, :], 1.0))
    P.on("vector", lambda e: e.memset(bscr[:, 0:2], 0.0))
    zo = P.ev("vector", s_misc, lambda e: e.engine_nop())

    for tt in range(NT):
        st_, si, g = x_stage.next()
        P.guard("gpsimd", g)
        v = P.dma("gpsimd", st_[:, :], x_in[tt * 128:(tt + 1) * 128, :], s_ld2)
        xb, xi, g2 = xb_tile.next()
        P.guard("scalar", g2)
        P.wait("scalar", s_ld2, v)
        cv = P.ev("scalar", s_cast, lambda e, o=xb, i=st_: e.copy(o[:, :], i[:, :]))
        x_stage.release(si, s_cast, cv)
        fs, fi, g4 = xfsl.next()
        P.guard("vector", g4)
        P.wait("tensor", s_cast, cv)
        last_ev = None
        for hc in range(ND):
            pb, pi, g3 = ps_tr.next()
            P.guard("tensor", g3)
            tv = P.ev("tensor", s_tr, lambda e, o=pb, i=xb, c=hc:
                      e.transpose(o[:, :], i[:, c * 128:(c + 1) * 128], ident_bf[:, :]))
            P.wait("vector", s_tr, tv)
            last_ev = P.ev("vector", s_ev, lambda e, o=fs, p=pb, c=hc:
                           e.tensor_copy(o[:, c * 128:(c + 1) * 128], p[:, :]))
            ps_tr.release(pi, s_ev, last_ev)
        xb_tile.release(xi, s_tr, s_tr.n)
        P.wait("sync", s_ev, last_ev)
        dv = None
        for hc in range(ND):
            dv = P.dma("sync", xf_d[hc, :, tt * 128:(tt + 1) * 128], fs[:, hc * 128:(hc + 1) * 128], s_qd)
        xfsl.release(fi, s_qd, dv)
    xf_written = s_qd.n

    def load_wsec(sec):
        wb, wi, g = wbuf.next()
        last = None
        for e16 in range(16):
            if g is not None:
                P.guard("sync", g)
            if last is not None:
                P.wait("sync", s_cast, last)
            r0, c0 = (e16 // 2) * 128, (e16 % 2) * 512
            src_ap = (wout_in[r0:r0 + 128, c0:c0 + 512] if sec == 3 else
                      wqkv_in[r0:r0 + 128, sec * D + c0:sec * D + c0 + 512])
            v = P.dma("sync", wstage[:, 0:512], src_ap, s_ld)
            P.wait("scalar", s_ld, v)
            last = P.ev("scalar", s_cast, lambda e, d=wb, s8=e16 // 2, c0=c0:
                        e.copy(d[:, s8 * 1024 + c0: s8 * 1024 + c0 + 512], wstage[:, 0:512]))
        return wb, wi, last

    def xsl_load(tc):
        fs, fi, g = xsl.next()
        P.guard("sync", g)
        dv = None
        for hc in range(ND):
            dv = P.dma("sync", fs[:, hc * 512:(hc + 1) * 512],
                       xf_d[hc, :, tc * 512:(tc + 1) * 512], s_ld3)
        return fs, fi, dv

    P.wait("sync", s_qd, xf_written)
    P.wait("vector", s_misc, zo)

    wb, wi_, wr = load_wsec(0)
    P.wait("tensor", s_cast, wr)
    qm_guard = 0
    for tc in range(NQC):
        fs, fi, fv = xsl_load(tc)
        P.wait("tensor", s_ld3, fv)
        tcs = slice(tc * 512, (tc + 1) * 512)
        for oc in range(ND):
            pb, pi, g = ps_proj.next()
            P.guard("tensor", g)
            for hc in range(ND):
                P.mm(pb[:, :], wb[:, hc * 1024 + oc * 128: hc * 1024 + oc * 128 + 128],
                     fs[:, hc * 512:(hc + 1) * 512], start=(hc == 0), stop=(hc == ND - 1),
                     sem=(s_mm if hc == ND - 1 else None))
            mv = s_mm.n
            P.wait("scalar", s_mm, mv)
            qb, qi, gq = qsb.next()
            P.guard("scalar", gq)
            cv = P.ev("scalar", s_cast, lambda e, o=qb, p=pb: e.copy(o[:, :], p[:, :]))
            ps_proj.release(pi, s_cast, cv)
            P.wait("vector", s_cast, cv)
            rdv = P.ev("vector", s_bnd2, lambda e, o=qb: e.tensor_reduce(
                bscr[:, 2:3], o[:, :], axis=AX.X, op=ALU.max, apply_absolute_value=True))
            P.wait("vector", s_bnd2, rdv)
            P.ev("vector", s_bnd2, lambda e: e.tensor_max(bscr[:, 0:1], bscr[:, 0:1], bscr[:, 2:3]))
            P.wait("tensor", s_cast, cv)
            if oc == 0:
                P.wait("tensor", s_ev, qm_guard)
            P.mm(ps_sm[:, :], selb[:, :], qb[:, :], start=(oc == 0), stop=(oc == ND - 1),
                 sem=(s_mm if oc == ND - 1 else None))
            P.wait("sync", s_cast, cv)
            dv = P.dma("sync", qd[oc, :, tcs], qb[:, :], s_qd)
            qsb.release(qi, s_qd, dv)
        qmv = s_mm.n
        P.wait("vector", s_mm, qmv)
        if tc > 0:
            P.wait("vector", s_qd, qm_dma)
        qm_guard = P.ev("vector", s_ev, lambda e: e.tensor_copy(qmsl[:, :], ps_sm[:, :]))
        P.wait("sync", s_ev, qm_guard)
        qm_dma = P.dma("sync", qm_d[:, tcs], qmsl[:, :], s_qd)
        xsl.release(fi, s_mm, qmv)
    q_done = s_ev.n
    wbuf.release(wi_, s_mm, s_mm.n)

    qm2_sb = P.sbuf("qm2_sb", [64, 512], F32, es=esS)
    hhat = P.sbuf("hhat", [64, 512], BF16, es=esS)
    h1b = hhat
    stat_a = P.sbuf("stat_a", [1, 512], F32, es=esS)
    stat_b = P.sbuf("stat_b", [1, 512], F32, es=esS)
    P.wait("sync", s_qd, s_qd.n)
    for tc in range(NQC):
        tcs = slice(tc * 512, (tc + 1) * 512)
        P.wait("scalar", s_g, s_g.n)
        P.wait("tensor", s_g, s_g.n)
        qv = P.dma("sync", qmsl[:, :], qm_d[:, tcs], s_ld3)
        P.wait("scalar", s_ld3, qv)
        sq = P.ev("scalar", s_g, lambda e: e.activation(qm2_sb[:, :], qmsl[:, :], AF.Square))
        P.wait("tensor", s_g, sq)
        P.wait("tensor", s_ld3, qv)
        P.wait("tensor", s_mm, s_mm.n)
        m1 = P.mm(ps_sm[0:1, :], ones_f32[0:64, 0:1], qmsl[:, :], start=True, stop=True, sem=s_mm)
        m2 = P.mm(ps_sm2[0:1, :], ones_f32[0:64, 0:1], qm2_sb[:, :], start=True, stop=True, sem=s_mm)
        P.wait("vector", s_mm, m2)
        for fn in [
            lambda e: e.tensor_scalar(stat_a[:, :], ps_sm[0:1, :], 1.0 / 64, None, ALU.mult),
            lambda e: e.tensor_scalar(stat_b[:, :], ps_sm2[0:1, :], 1.0 / 64, None, ALU.mult),
            lambda e: e.tensor_mul(srow[:, :], stat_a[:, :], stat_a[:, :]),
            lambda e: e.tensor_sub(srow[:, :], stat_b[:, :], srow[:, :]),
            lambda e: e.tensor_scalar(srow[:, :], srow[:, :], 1e-5, None, ALU.add),
        ]:
            v1 = P.ev("vector", s_g, fn)
            P.wait("vector", s_g, v1)
        P.wait("scalar", s_g, s_g.n)
        v1 = P.ev("scalar", s_g, lambda e: e.activation(stat_b[:, :], srow[:, :], AF.Sqrt))
        P.wait("vector", s_g, v1)
        for fn in [
            lambda e: e.reciprocal(srow[:, :], stat_b[:, :]),
            lambda e: e.tensor_mul(stat_a[:, :], stat_a[:, :], srow[:, :]),
            lambda e: e.tensor_scalar(stat_a[:, :], stat_a[:, :], -1.0, None, ALU.mult),
        ]:
            v1 = P.ev("vector", s_g, fn)
            P.wait("vector", s_g, v1)
        P.wait("tensor", s_g, s_g.n)
        a1 = P.ev("tensor", s_mm, lambda e: e.matmul(ps_sm[:, :], ones_f32[0:1, 0:64], srow[:, :], start=True, stop=True))
        a2 = P.ev("tensor", s_mm, lambda e: e.matmul(ps_sm2[:, :], ones_f32[0:1, 0:64], stat_a[:, :], start=True, stop=True))
        P.wait("vector", s_mm, a2)
        for fn in [
            lambda e: e.tensor_mul(qm2_sb[:, :], qmsl[:, :], ps_sm[:, :]),
            lambda e: e.tensor_add(qm2_sb[:, :], qm2_sb[:, :], ps_sm2[:, :]),
            lambda e: e.tensor_scalar(hhat[:, :], qm2_sb[:, :], somap[:, 0:1], somap[:, 1:2], ALU.mult, ALU.add),
        ]:
            v1 = P.ev("vector", s_g, fn)
            P.wait("vector", s_g, v1)
        P.wait("tensor", s_g, s_g.n)
        P.mm(ps_sm[0:16, :], w1sb[:, :], hhat[:, :], start=True, stop=False)
        h2 = P.mm(ps_sm[0:16, :], b1sb[:, :], ones_bf[:, :], start=False, stop=True, sem=s_mm)
        P.wait("scalar", s_mm, h2)
        g1 = P.ev("scalar", s_g, lambda e: e.activation(h1b[0:16, :], ps_sm[0:16, :], AF.Gelu))
        P.wait("tensor", s_g, g1)
        P.mm(ps_sm2[0:1, :], w2sb[:, :], h1b[0:16, :], start=True, stop=False)
        sc2 = P.mm(ps_sm2[0:1, :], b2sb[:, :], ones_bf[:, :], start=False, stop=True, sem=s_mm)
        P.wait("vector", s_mm, sc2)
        v1 = P.ev("vector", s_g, lambda e: e.tensor_copy(srow[:, :], ps_sm2[0:1, :]))
        P.wait("tensor", s_g, v1)
        for j in range(4):
            tv = P.ev("tensor", s_bnd, lambda e, j=j: e.transpose(
                ps_ct[:, 0:1], srow[:, j * 128:(j + 1) * 128], ident_f[0:1, 0:1]))
            P.wait("vector", s_bnd, tv)
            cv = P.ev("vector", s_bnd, lambda e, c=tc * 4 + j: e.tensor_copy(scol[:, c:c + 1], ps_ct[:, 0:1]))
            P.wait("tensor", s_bnd, cv)
        P.wait("vector", s_bnd, s_bnd.n)

    iv = P.ev("vector", s_bnd, lambda e: e.tensor_reduce(bscr[:, 0:1], scol[:, :], axis=AX.X, op=ALU.max))
    P.wait("vector", s_bnd, iv)
    iv = P.ev("vector", s_bnd, lambda e: e.tensor_reduce(bscr[:, 32:33], scol[:, :], axis=AX.X, op=ALU.min))
    P.wait("tensor", s_bnd, iv)
    tv = P.ev("tensor", s_bnd, lambda e: e.transpose(ps_c1[:, :], bscr[:, 0:64], ident_f[:, :]))
    P.wait("vector", s_bnd, tv)
    iv = P.ev("vector", s_bnd, lambda e: e.tensor_reduce(bscr[0:1, 8:9], ps_c1[0:1, 0:128], axis=AX.X, op=ALU.max))
    P.wait("vector", s_bnd, iv)
    iv = P.ev("vector", s_bnd, lambda e: e.tensor_scalar(bscr[0:1, 8:9], bscr[0:1, 8:9], 1.0, None, ALU.add))
    P.wait("vector", s_bnd, iv)
    iv = P.ev("vector", s_bnd, lambda e: e.tensor_reduce(bscr[32:33, 8:9], ps_c1[32:33, 0:128], axis=AX.X, op=ALU.min))
    P.wait("vector", s_bnd, iv)
    iv = P.ev("vector", s_bnd, lambda e: e.tensor_scalar(bscr[32:33, 8:9], bscr[32:33, 8:9], -1.0, None, ALU.add))
    P.wait("gpsimd", s_bnd, iv)
    mv0 = P.dma("gpsimd", bscr[0:1, 10:11], bscr[32:33, 8:9], s_mv)
    P.wait("tensor", s_mv, mv0)
    P.wait("tensor", s_bnd, s_bnd.n)
    tv = P.ev("tensor", s_bnd, lambda e: e.matmul(ps_ct[:, 0:2], ones_f32[0:1, 0:128], bscr[0:1, 8:11:2], start=True, stop=True))
    P.wait("vector", s_bnd, tv)
    iv = P.ev("vector", s_bnd, lambda e: e.tensor_copy(bsc[:, 1:2], ps_ct[:, 0:1]))
    P.wait("vector", s_bnd, iv)
    iv = P.ev("vector", s_bnd, lambda e: e.tensor_copy(bsc[:, 0:1], ps_ct[:, 1:2]))
    P.wait("vector", s_bnd, iv)
    for it in range(42):
        for fn in [
            lambda e: e.tensor_add(bsc[:, 4:5], bsc[:, 0:1], bsc[:, 1:2]),
            lambda e: e.tensor_scalar(bsc[:, 4:5], bsc[:, 4:5], 0.5, None, ALU.mult),
            lambda e: e.tensor_scalar(cmp32[:, :], scol[:, :], bsc[:, 4:5], None, ALU.is_ge),
            lambda e: e.tensor_reduce(bsc[:, 6:7], cmp32[:, :], axis=AX.X, op=ALU.add),
        ]:
            c1 = P.ev("vector", s_bnd, fn)
            P.wait("vector", s_bnd, c1)
        P.wait("tensor", s_bnd, s_bnd.n)
        t1 = P.ev("tensor", s_bnd2, lambda e: e.matmul(ps_c1[0:1, 0:1], ones_f32[:, 0:1], bsc[:, 6:7], start=True, stop=True))
        P.wait("vector", s_bnd2, t1)
        c1 = P.ev("vector", s_bnd, lambda e: e.tensor_copy(bscr[0:1, 12:13], ps_c1[0:1, 0:1]))
        P.wait("tensor", s_bnd, c1)
        t1 = P.ev("tensor", s_bnd2, lambda e: e.matmul(ps_ct[:, 2:3], ones_f32[0:1, 0:128], bscr[0:1, 12:13], start=True, stop=True))
        P.wait("vector", s_bnd2, t1)
        for fn in [
            lambda e: e.tensor_scalar(bsc[:, 7:8], ps_ct[:, 2:3], 256.0, None, ALU.is_ge),
            lambda e: e.tensor_sub(bsc[:, 2:3], bsc[:, 4:5], bsc[:, 0:1]),
            lambda e: e.tensor_mul(bsc[:, 2:3], bsc[:, 2:3], bsc[:, 7:8]),
            lambda e: e.tensor_add(bsc[:, 0:1], bsc[:, 0:1], bsc[:, 2:3]),
            lambda e: e.tensor_scalar(bsc[:, 7:8], bsc[:, 7:8], -1.0, 1.0, ALU.mult, ALU.add),
            lambda e: e.tensor_sub(bsc[:, 3:4], bsc[:, 4:5], bsc[:, 1:2]),
            lambda e: e.tensor_mul(bsc[:, 3:4], bsc[:, 3:4], bsc[:, 7:8]),
            lambda e: e.tensor_add(bsc[:, 1:2], bsc[:, 1:2], bsc[:, 3:4]),
        ]:
            c1 = P.ev("vector", s_bnd, fn)
            P.wait("vector", s_bnd, c1)
    P.wait("scalar", s_bnd, s_bnd.n)
    gv = P.ev("scalar", s_g, lambda e: e.activation(sig32[:, :], scol[:, :], AF.Sigmoid))
    P.wait("vector", s_g, gv)
    for fn in [
        lambda e: e.tensor_scalar(cmp32[:, :], scol[:, :], bsc[:, 0:1], None, ALU.is_ge),
        lambda e: e.tensor_mul(sig32[:, :], sig32[:, :], cmp32[:, :]),
    ]:
        gv = P.ev("vector", s_g, fn)
        P.wait("vector", s_g, gv)
    P.wait("tensor", s_g, s_g.n)
    for tc in range(NQC):
        tcs = slice(tc * 512, (tc + 1) * 512)
        for j in range(4):
            P.wait("tensor", s_g, s_g.n)
            tv = P.ev("tensor", s_g, lambda e, c=tc * 4 + j: e.transpose(
                ps_c1[0:1, 0:128], sig32[:, c:c + 1], ident_f[:, :]))
            P.wait("vector", s_g, tv)
            cv = P.ev("vector", s_g, lambda e, j=j: e.tensor_copy(growsl[:, j * 128:(j + 1) * 128], ps_c1[0:1, 0:128]))
            P.wait("tensor", s_g, cv)
        pb, pi, g = ps_proj.next()
        P.guard("tensor", g)
        P.mm(pb[0:64, :], ones_f32[0:1, 0:64], growsl[:, :], start=True, stop=True, tile_position=(0, 0))
        bv = P.mm(pb[64:128, :], ones_f32[0:1, 64:128], growsl[:, :], start=True, stop=True,
                  sem=s_g, tile_position=(0, 64))
        P.wait("vector", s_g, bv)
        cv = P.ev("vector", s_g, lambda e, t=tcs, p=pb: e.tensor_copy(grep[:, t], p[:, :]))
        ps_proj.release(pi, s_g, cv)
        P.wait("tensor", s_g, cv)
    gates_done = s_g.n
    esS.close()
    ps_sq = P.psum("ps_sq", [128, 512], F32)

    ksb = Rot([P.sbuf(f"ksb{i}", [128, 512], BF16) for i in range(1)])
    P.wait("vector", s_g, gates_done)
    for which in (1, 2):
        wb2, wi2, wr2 = load_wsec(which)
        P.wait("tensor", s_cast, wr2)
        for tc in range(NQC):
            fs, fi, fv = xsl_load(tc)
            P.wait("tensor", s_ld3, fv)
            tcs = slice(tc * 512, (tc + 1) * 512)
            if which == 1:
                for oc in range(ND):
                    pb, pi, g = ps_proj.next()
                    P.guard("tensor", g)
                    for hc in range(ND):
                        P.mm(pb[:, :], wb2[:, hc * 1024 + oc * 128: hc * 1024 + oc * 128 + 128],
                             fs[:, hc * 512:(hc + 1) * 512], start=(hc == 0), stop=(hc == ND - 1),
                             sem=(s_mm if hc == ND - 1 else None))
                    mv = s_mm.n
                    P.wait("scalar", s_mm, mv)
                    kb, ki, gk = ksb.next()
                    P.guard("scalar", gk)
                    cv = P.ev("scalar", s_cast, lambda e, o=kb, p=pb: e.copy(o[:, :], p[:, :]))
                    ps_proj.release(pi, s_cast, cv)
                    P.wait("vector", s_cast, cv)
                    rdv = P.ev("vector", s_bnd2, lambda e, o=kb: e.tensor_reduce(
                        bscr[:, 2:3], o[:, :], axis=AX.X, op=ALU.max, apply_absolute_value=True))
                    P.wait("vector", s_bnd2, rdv)
                    P.ev("vector", s_bnd2, lambda e: e.tensor_max(bscr[:, 0:1], bscr[:, 0:1], bscr[:, 2:3]))
                    fv2 = P.ev("vector", s_cast, lambda e, o=ktb[oc], t=tcs, i=kb:
                               e.tensor_mul(o[:, t], i[:, :], grep[:, t]))
                    ksb.release(ki, s_cast, fv2)
            else:
                for j in range(4):
                    tt = tc * 4 + j
                    pbs = []
                    for half in range(2):
                        pb, pi, g = ps_proj.next()
                        P.guard("tensor", g)
                        for hc in range(ND):
                            P.mm(pb[:, :], fs[:, hc * 512 + j * 128: hc * 512 + j * 128 + 128],
                                 wb2[:, hc * 1024 + half * 512: hc * 1024 + (half + 1) * 512],
                                 start=(hc == 0), stop=(hc == ND - 1),
                                 sem=(s_mm if hc == ND - 1 else None))
                        pbs.append((pb, pi, s_mm.n))
                    P.wait("scalar", s_mm, pbs[-1][2])
                    dst = vaug[tt].ap().rearrange("p (h c) -> p h c", c=65)
                    for half, (pb, pi, mv) in enumerate(pbs):
                        vv = P.ev("scalar", s_misc, lambda e, d=dst, p=pb, hf=half:
                                  e.copy(d[:, hf * 8:(hf + 1) * 8, 0:64],
                                         p[:, :].rearrange("p (h c) -> p h c", c=64)))
                        ps_proj.release(pi, s_misc, vv)
            xsl.release(fi, s_mm, s_mm.n)
        wbuf.release(wi2, s_mm, s_mm.n)
    v_done = s_misc.n
    P.wait("vector", s_misc, v_done)
    for tt in range(NT):
        dst = vaug[tt].ap().rearrange("p (h c) -> p h c", c=65)
        P.on("vector", lambda e, d=dst: e.memset(d[:, :, 64:65], 1.0))
    vones = P.ev("vector", s_misc, lambda e: e.engine_nop())

    wb_o, wi_o, wr_o = load_wsec(3)

    P.wait("vector", s_bnd2, s_bnd2.n)
    P.wait("tensor", s_mm, s_mm.n)
    ncv = emit_negC(P, bscr, negC, mult=8.0, clamp=60.0, s_chain=s_bnd,
                    s_extra=s_bnd2, bc_ps=ps_proj.bufs[0], tr_ps=ps_wo.bufs[0], ident_f=ident_f)
    P.wait("scalar", s_bnd, ncv)
    P.wait("tensor", s_misc, vones)
    P.wait("scalar", s_misc, vones)

    ctx = {"xr_rot": x_stage, "lnt": lnt, "ps_sq": ps_sq,
           "s_ld2": s_ld2, "s_ln": s_ln, "s_st": s_st}

    qsl = xsl

    def get_q(qc):
        fs, fi, g = qsl.next()
        P.guard("sync", g)
        dv = None
        for c in range(ND):
            dv = P.dma("sync", fs[:, c * 512:(c + 1) * 512], qd[c, :, qc * 512:(qc + 1) * 512], s_ld3)
        P.wait("vector", s_ld3, dv)
        fv = None
        for c in range(ND):
            fv = P.ev("vector", s_cast, lambda e, f=fs, c=c, q=qc: e.tensor_mul(
                f[:, c * 512:(c + 1) * 512], f[:, c * 512:(c + 1) * 512],
                grep[:, q * 512:(q + 1) * 512]))
        chunks = [fs[:, c * 512:(c + 1) * 512] for c in range(ND)]
        qsl._pending = fi
        return chunks, (s_cast, fv)

    def after_qc(qc, evc_val, mv_val):
        qsl.release(qsl._pending, s_mm, s_mm.n)
        P.wait("tensor", s_evc, evc_val)
        P.wait("tensor", s_mv, mv_val)
        P.wait("tensor", s_cast, wr_o)
        wo_last = 0
        for oc in range(ND):
            pb, pi, g = ps_wo.next()
            P.guard("tensor", g)
            for dc in range(ND):
                P.mm(pb[:, :], wb_o[:, dc * 1024 + oc * 128: dc * 1024 + oc * 128 + 128],
                     yatt[dc][:, :], start=(dc == 0), stop=False)
            mv = P.mm(pb[:, :], biases[:, oc * 128: oc * 128 + 128],
                      ones_bf[:, :], start=False, stop=True, sem=s_wo)
            wo_last = mv
            P.wait("scalar", s_wo, mv)
            yb, yi, g2 = ytmp.next()
            P.guard("scalar", g2)
            cv = P.ev("scalar", s_wo, lambda e, y=yb, p=pb: e.copy(y[:, :], p[:, :]))
            ps_wo.release(pi, s_wo, cv)
            P.wait("tensor", s_wo, cv)
            lastt = None
            for j in range(4):
                pt_, pti, g3 = ps_tr.next()
                P.guard("tensor", g3)
                tv = P.ev("tensor", s_tr, lambda e, o=pt_, i=yb, j=j:
                          e.transpose(o[:, :], i[:, j * 128:(j + 1) * 128], ident_bf[:, :]))
                P.wait("vector", s_tr, tv)
                evv = P.ev("vector", s_res, lambda e, o=y_blk[j], p=pt_, c=oc:
                           e.tensor_copy(o[:, c * 128:(c + 1) * 128], p[:, :]))
                ps_tr.release(pti, s_res, evv)
                lastt = tv
            ytmp.release(yi, s_tr, lastt)
        yblk_done = s_res.n
        P.wait("vector", s_res, yblk_done)
        for j in range(4):
            emit_postln(P, ctx, qc * 4 + j, y_blk[j][:, :], x_in, y_out)
        return (s_wo, wo_last)

    emit_attention(P, dict(
        tag="c", ktb=ktb, vaug=vaug, yatt=yatt, nheads=16, causal=False, dh=64,
        negC=negC, scale=0.125, get_q=get_q, after_qc=after_qc,
        s_mm=s_mm, s_exp=s_exp, s_pv=s_pv, s_evc=s_evc, s_msk=s_msk, s_mv=s_mv))

    P.wait("gpsimd", s_st, s_st.n)
    return P


def _prep_l1_attn_inputs(x_full, w_qkv, w_out, b_out, ln_g, ln_b, w1, b1, w2, b2):
    consts = host_consts()
    sel = np.zeros((128, 64), dtype=np.float32)
    for a in range(2):
        for j in range(64):
            sel[a * 64 + j, j] = 1.0 / 16.0
    base = {
        "wqkv_in": np.ascontiguousarray(w_qkv.T),
        "wout_in": np.ascontiguousarray(w_out.T),
        "bout_in": np.ascontiguousarray(b_out)[None, :],
        "lng_in": np.ascontiguousarray(ln_g)[:, None],
        "lnb_in": np.ascontiguousarray(ln_b)[:, None],
        "w1s_in": np.ascontiguousarray(w1.T),
        "b1s_in": np.ascontiguousarray(b1)[None, :],
        "w2s_in": np.ascontiguousarray(w2.T),
        "b2s_in": np.ascontiguousarray(b2)[None, :],
        "sel_in": sel,
        "ident_in": consts["ident"],
    }
    in_maps = []
    for c in range(NCORES):
        m = dict(base)
        m["x_in"] = np.ascontiguousarray(x_full[c])
        in_maps.append(m)
    return in_maps


def run_l1_attn(x_full, w_qkv, w_out, b_out, ln_g, ln_b, w1, b1, w2, b2):
    in_maps = _prep_l1_attn_inputs(x_full, w_qkv, w_out, b_out, ln_g, ln_b, w1, b1, w2, b2)
    P = build_prog_l1_attn()
    nc = P.finalize()
    res = run_bass_kernel_spmd(nc, in_maps, core_ids=list(range(NCORES)))
    return np.stack([r["y_out"] for r in res.results], axis=0)



def build_prog_l1_ffn():
    P = Prog("D")
    nc = P.nc
    NS = 64
    DQ = 256

    x_in = nc.dram_tensor("x_in", [T, D], F32, kind="ExternalInput").ap()
    sinit_in = nc.dram_tensor("sinit_in", [NS, D], F32, kind="ExternalInput").ap()
    w1_in = nc.dram_tensor("w1_in", [2 * D, 512], F32, kind="ExternalInput").ap()
    b1_in = nc.dram_tensor("b1_in", [512, 1], F32, kind="ExternalInput").ap()
    w2_in = nc.dram_tensor("w2_in", [512, NS], F32, kind="ExternalInput").ap()
    b2_in = nc.dram_tensor("b2_in", [NS, 1], F32, kind="ExternalInput").ap()
    tau_in = nc.dram_tensor("tau_in", [1, 1], F32, kind="ExternalInput").ap()
    wih_in = nc.dram_tensor("wih_in", [D, 3 * D], F32, kind="ExternalInput").ap()
    whh_in = nc.dram_tensor("whh_in", [D, 3 * D], F32, kind="ExternalInput").ap()
    bih_in = nc.dram_tensor("bih_in", [1, 3 * D], F32, kind="ExternalInput").ap()
    bhh_in = nc.dram_tensor("bhh_in", [1, 3 * D], F32, kind="ExternalInput").ap()
    wq_in = nc.dram_tensor("wq_in", [D, DQ], F32, kind="ExternalInput").ap()
    bq_in = nc.dram_tensor("bq_in", [1, DQ], F32, kind="ExternalInput").ap()
    wk_in = nc.dram_tensor("wk_in", [D, DQ], F32, kind="ExternalInput").ap()
    bk_in = nc.dram_tensor("bk_in", [1, DQ], F32, kind="ExternalInput").ap()
    wv_in = nc.dram_tensor("wv_in", [D, D], F32, kind="ExternalInput").ap()
    bv_in = nc.dram_tensor("bv_in", [1, D], F32, kind="ExternalInput").ap()
    lng_in = nc.dram_tensor("lng_in", [1, D], F32, kind="ExternalInput").ap()
    lnb_in = nc.dram_tensor("lnb_in", [1, D], F32, kind="ExternalInput").ap()
    wvp_in = nc.dram_tensor("wvp_in", [D, D], F32, kind="ExternalInput").ap()
    bvp_in = nc.dram_tensor("bvp_in", [1, D], F32, kind="ExternalInput").ap()
    wo_in = nc.dram_tensor("wo_in", [D, D], F32, kind="ExternalInput").ap()
    bo_in = nc.dram_tensor("bo_in", [1, D], F32, kind="ExternalInput").ap()
    ident_in = nc.dram_tensor("ident_in", [128, 128], F32, kind="ExternalInput").ap()
    y_out = nc.dram_tensor("y_out", [T, D], F32, kind="ExternalOutput").ap()
    xf_d = nc.dram_tensor("xf_d", [ND, 128, T], BF16).ap()

    ident_f = P.sbuf("ident_f", [128, 128], F32)
    ident_bf = P.sbuf("ident_bf", [128, 128], BF16)
    ones_f32 = P.sbuf("ones_f32", [128, 128], F32)
    P._ones_f32 = ones_f32
    ones_bf = P.sbuf("ones_bf", [1, 512], BF16)
    x_tok1 = [P.sbuf(f"x_tok1_{t}", [128, D + 1], BF16) for t in range(NT)]
    x_stage = Rot([P.sbuf(f"x_stage{i}", [128, D], F32) for i in range(2)])
    xfsl = Rot([P.sbuf(f"xfsl{i}", [128, D], BF16) for i in range(2)])
    xsl = Rot([P.sbuf(f"xsl{i}", [128, 8 * 512], BF16) for i in range(1)])
    wstage = P.sbuf("wstage", [128, 1024], F32)
    wbuf = Rot([P.sbuf(f"wbuf{i}", [128, 8192], BF16) for i in range(1)])
    w1ab = P.sbuf("w1ab", [128, 8 * 512], BF16)
    w2b = P.sbuf("w2b", [128, 4 * 64], BF16)
    hTc = [P.sbuf(f"hTc{i}", [128, 512], BF16) for i in range(4)]
    gbias = P.sbuf("gbias", [128, 8], F32)
    b1c = P.sbuf("b1c", [128, 4], F32)
    msT = P.sbuf("msT", [128, 8], F32)
    mu_bf = P.sbuf("mu_bf", [128, 8], BF16)
    w1btmp = Rot([P.sbuf(f"w1btmp{i}", [128, 512], BF16) for i in range(2)])
    rls = P.sbuf("rls", [64, 512], F32)
    rl_t = Rot([P.sbuf(f"rl_t{i}", [128, 64], F32) for i in range(2)])
    m8 = P.sbuf("m8", [128, 12], F32)
    e8t = P.sbuf("e8t", [128, 64], F32)
    alphab = [P.sbuf(f"alphab{t}", [128, 64], BF16) for t in range(NT)]
    alphaF = P.sbuf("alphaF", [64, T], BF16)
    sinitT = P.sbuf("sinitT", [128, 8 * 64], BF16)
    sinitTf = P.sbuf("sinitTf", [128, 8 * 64], F32)
    S0 = P.sbuf("S0", [64, D], F32)
    srow = P.sbuf("srow", [64, D], F32)
    g2s = P.sbuf("g2s", [64, D], F32)
    r_g = P.sbuf("r_g", [64, D], F32)
    z_g = P.sbuf("z_g", [64, D], F32)
    Snew = P.sbuf("Snew", [64, D], F32)
    sb64 = P.sbuf("sb64", [64, D], BF16)
    sF = P.sbuf("sF", [128, 8 * 64], BF16)
    qs = P.sbuf("qs", [64, DQ], BF16)
    ks = P.sbuf("ks", [64, DQ], BF16)
    vs = P.sbuf("vs", [64, D], BF16)
    qT = P.sbuf("qT", [128, 2 * 64], BF16)
    kT = P.sbuf("kT", [128, 2 * 64], BF16)
    pat = P.sbuf("pat", [64, 64], BF16)
    patT = P.sbuf("patT", [64, 64], BF16)
    escr = P.sbuf("escr", [64, 64], F32)
    lsc = P.sbuf("lsc", [64, 8], F32)
    svb = P.sbuf("svb", [64, D], BF16)
    gfb = [P.sbuf(f"gfb{c}", [128, 512], BF16) for c in range(ND)]
    ytmp = Rot([P.sbuf(f"ytmp{i}", [128, 512], BF16) for i in range(1)])
    y_blk = [P.sbuf(f"y_blk{i}", [128, D], BF16) for i in range(4)]
    lnt = [P.sbuf(f"lnt{i}", [128, 8], F32) for i in range(2)]
    birow = P.sbuf("birow", [1, D], BF16)
    bhrow = P.sbuf("bhrow", [1, D], BF16)
    rows2 = P.sbuf("rows2", [1, 4608 + D], BF16)
    tinyf = P.sbuf("tinyf", [1, 8], F32)

    ps_tr = Rot([P.psum("ps_tr0", [128, 128], BF16)])
    ps_a = Rot([P.psum(f"ps_a{i}", [128, 512], F32) for i in range(1)])
    ps_b = P.psum("ps_b", [64, 1024], F32)
    ps_c = P.psum("ps_c", [64, 1024], F32)
    ps_d = P.psum("ps_d", [64, 4], F32)
    ps_sq = P.psum("ps_sq", [128, 512], F32)
    ps_trf = Rot([ps_sq])

    s_ld = P.sem(); s_ld2 = P.sem(); s_ld3 = P.sem(); s_cast = P.sem(); s_tr = P.sem()
    s_ev = P.sem(); s_mm = P.sem(); s_r = P.sem(); s_g = P.sem(); s_ln = P.sem()
    s_st = P.sem(); s_misc = P.sem(); s_qd = P.sem()

    idv = P.dma("sync", ident_f[:, :], ident_in, s_ld)
    P.wait("scalar", s_ld, idv)
    P.ev("scalar", s_cast, lambda e: e.copy(ident_bf[:, :], ident_f[:, :]))
    P.on("vector", lambda e: e.memset(ones_f32[:, :], 1.0))
    P.on("vector", lambda e: e.memset(ones_bf[:, :], 1.0))
    zo = P.ev("vector", s_misc, lambda e: e.engine_nop())
    packs = [(bq_in, 0, DQ), (bk_in, DQ, DQ), (bv_in, 512, D), (lng_in, 1536, D),
             (lnb_in, 2560, D), (bvp_in, 3584, D), (bo_in, 4608, D)]
    last = None
    for src_ap, off, w in packs:
        if last is not None:
            P.wait("sync", s_cast, last)
        v = P.dma("sync", wstage[0:1, 0:w], src_ap, s_ld)
        P.wait("scalar", s_ld, v)
        last = P.ev("scalar", s_cast, lambda e, o=off, w=w: e.copy(rows2[:, o:o + w], wstage[0:1, 0:w]))
    for i4 in range(4):
        P.dma("sync", b1c[:, i4:i4 + 1], b1_in[i4 * 128:(i4 + 1) * 128, :], s_ld)
    P.dma("sync", gbias[0:64, 4:5], b2_in, s_ld)
    tv0 = P.dma("sync", tinyf[0:1, 0:1], tau_in, s_ld)
    P.wait("vector", s_ld, s_ld.n)
    for fn in [
        lambda e: e.tensor_scalar(tinyf[0:1, 1:2], tinyf[0:1, 0:1], -1.0, None, ALU.mult),
        lambda e: e.tensor_max(tinyf[0:1, 2:3], tinyf[0:1, 0:1], tinyf[0:1, 1:2]),
        lambda e: e.tensor_scalar(tinyf[0:1, 2:3], tinyf[0:1, 2:3], 0.1, None, ALU.add),
        lambda e: e.reciprocal(tinyf[0:1, 3:4], tinyf[0:1, 2:3]),
    ]:
        vv = P.ev("vector", s_misc, fn)
        P.wait("vector", s_misc, vv)
    P.wait("tensor", s_misc, s_misc.n)
    pb0 = ps_trf.bufs[0]
    tb = P.ev("tensor", s_mm, lambda e: e.matmul(pb0[0:64, 0:1], ones_f32[0:1, 0:64], tinyf[0:1, 3:4], start=True, stop=True))
    P.wait("vector", s_mm, tb)
    tv2 = P.ev("vector", s_misc, lambda e: e.tensor_copy(gbias[0:64, 5:6], pb0[0:64, 0:1]))
    P.wait("tensor", s_misc, tv2)

    for tt in range(NT):
        st_, si_, g = x_stage.next()
        P.guard("gpsimd", g)
        v = P.dma("gpsimd", st_[:, :], x_in[tt * 128:(tt + 1) * 128, :], s_ld2)
        P.wait("scalar", s_ld2, v)
        cv = P.ev("scalar", s_cast, lambda e, o=x_tok1[tt], i=st_: e.copy(o[:, 0:D], i[:, :]))
        x_stage.release(si_, s_cast, cv)
        P.on("vector", lambda e, o=x_tok1[tt]: e.memset(o[:, D:D + 1], 1.0))
        fs, fi, g4 = xfsl.next()
        P.guard("vector", g4)
        P.wait("tensor", s_cast, cv)
        last_ev = None
        for hc in range(ND):
            pb, pi, g3 = ps_tr.next()
            P.guard("tensor", g3)
            tv = P.ev("tensor", s_tr, lambda e, o=pb, i=x_tok1[tt], c=hc:
                      e.transpose(o[:, :], i[:, c * 128:(c + 1) * 128], ident_bf[:, :]))
            P.wait("vector", s_tr, tv)
            last_ev = P.ev("vector", s_ev, lambda e, o=fs, p=pb, c=hc:
                           e.tensor_copy(o[:, c * 128:(c + 1) * 128], p[:, :]))
            ps_tr.release(pi, s_ev, last_ev)
        P.wait("sync", s_ev, last_ev)
        dv = None
        for hc in range(ND):
            dv = P.dma("sync", xf_d[hc, :, tt * 128:(tt + 1) * 128], fs[:, hc * 128:(hc + 1) * 128], s_qd)
        xfsl.release(fi, s_qd, dv)
    xf_written = s_qd.n

    v = P.dma("sync", S0[:, :], sinit_in, s_ld)
    P.wait("tensor", s_ld, v)
    lastt = None
    for dc in range(ND):
        pb, pi, g = ps_trf.next()
        P.guard("tensor", g)
        tv = P.ev("tensor", s_tr, lambda e, o=pb, c=dc: e.transpose(
            o[:, 0:64], S0[:, c * 128:(c + 1) * 128], ident_f[0:64, 0:64]))
        P.wait("vector", s_tr, tv)
        ev = P.ev("vector", s_ev, lambda e, p=pb, c=dc: e.tensor_copy(sinitTf[:, c * 64:(c + 1) * 64], p[:, 0:64]))
        ps_trf.release(pi, s_ev, ev)
        P.wait("vector", s_ev, ev)
        ev = P.ev("vector", s_ev, lambda e, c=dc: e.tensor_reduce(
            msT[:, c:c + 1], sinitTf[:, c * 64:(c + 1) * 64], axis=AX.X, op=ALU.add))
        P.wait("vector", s_ev, ev)
        lastt = ev
    ev = P.ev("vector", s_ev, lambda e: e.tensor_scalar(msT[:, :], msT[:, :], 1.0 / NS, None, ALU.mult))
    P.wait("scalar", s_ev, ev)
    c1 = P.ev("scalar", s_cast, lambda e: e.copy(sinitT[:, :], sinitTf[:, :]))
    c2 = P.ev("scalar", s_cast, lambda e: e.copy(mu_bf[:, :], msT[:, :]))
    P.wait("tensor", s_cast, c2)

    last = None
    for e8i in range(8):
        if last is not None:
            P.wait("sync", s_cast, last)
        v = P.dma("sync", wstage[:, 0:512], w1_in[e8i * 128:(e8i + 1) * 128, :], s_ld)
        P.wait("scalar", s_ld, v)
        last = P.ev("scalar", s_cast, lambda e, i=e8i: e.copy(w1ab[:, i * 512:(i + 1) * 512], wstage[:, 0:512]))
    for i4 in range(4):
        P.wait("sync", s_cast, s_cast.n)
        v = P.dma("sync", wstage[:, 0:64], w2_in[i4 * 128:(i4 + 1) * 128, :], s_ld)
        P.wait("scalar", s_ld, v)
        P.ev("scalar", s_cast, lambda e, i=i4: e.copy(w2b[:, i * 64:(i + 1) * 64], wstage[:, 0:64]))
    pb, pi, gps = ps_a.next()
    P.guard("tensor", gps)
    for dc in range(ND):
        wt, wi, gw = w1btmp.next()
        P.wait("sync", s_cast, s_cast.n)
        P.guard("sync", gw)
        v = P.dma("sync", wstage[:, 0:512], w1_in[D + dc * 128:D + (dc + 1) * 128, :], s_ld)
        P.wait("scalar", s_ld, v)
        cv = P.ev("scalar", s_cast, lambda e, o=wt: e.copy(o[:, :], wstage[:, 0:512]))
        P.wait("tensor", s_cast, cv)
        for dhc in range(4):
            P.mm(pb[:, dhc:dhc + 1], wt[:, dhc * 128:(dhc + 1) * 128], mu_bf[:, dc:dc + 1],
                 start=(dc == 0), stop=(dc == ND - 1),
                 sem=(s_mm if dhc == 3 else None))
        w1btmp.release(wi, s_mm, s_mm.n)
    P.wait("vector", s_mm, s_mm.n)
    P.wait("vector", s_ld, s_ld.n)
    gv = P.ev("vector", s_ev, lambda e, p=pb: e.tensor_add(gbias[:, 0:4], p[:, 0:4], b1c[:, :]))
    ps_a.release(pi, s_ev, gv)
    P.wait("scalar", s_ev, gv)
    P.wait("tensor", s_ev, gv)

    def xsl_load(tc):
        fs, fi, g = xsl.next()
        P.guard("sync", g)
        dv = None
        for hc in range(ND):
            dv = P.dma("sync", fs[:, hc * 512:(hc + 1) * 512],
                       xf_d[hc, :, tc * 512:(tc + 1) * 512], s_ld3)
        return fs, fi, dv

    P.wait("sync", s_qd, xf_written)
    for tc in range(NQC):
        fs, fi, fv = xsl_load(tc)
        P.wait("tensor", s_ld3, fv)
        for dhc in range(4):
            pb, pi, g = ps_a.next()
            P.guard("tensor", g)
            for dc in range(ND):
                P.mm(pb[:, :], w1ab[:, dc * 512 + dhc * 128: dc * 512 + dhc * 128 + 128],
                     fs[:, dc * 512:(dc + 1) * 512], start=(dc == 0), stop=(dc == ND - 1),
                     sem=(s_mm if dc == ND - 1 else None))
            P.wait("scalar", s_mm, s_mm.n)
            hv = P.ev("scalar", s_g, lambda e, o=hTc[dhc], p=pb, d=dhc: e.activation(
                o[:, :], p[:, :], AF.Gelu, bias=gbias[:, d:d + 1]))
            ps_a.release(pi, s_g, hv)
        P.wait("tensor", s_g, s_g.n)
        P.wait("tensor", s_r, s_r.n)
        for dhc in range(4):
            P.mm(ps_b[:, 0:512], w2b[:, dhc * 64:(dhc + 1) * 64], hTc[dhc][:, :],
                 start=(dhc == 0), stop=(dhc == 3), sem=(s_mm if dhc == 3 else None))
        P.wait("vector", s_mm, s_mm.n)
        P.wait("vector", s_tr, s_tr.n)
        rv = P.ev("vector", s_r, lambda e: e.tensor_scalar(
            rls[:, :], ps_b[0:64, 0:512], gbias[0:64, 4:5], gbias[0:64, 5:6], ALU.add, ALU.mult))
        P.wait("tensor", s_r, rv)
        for j in range(4):
            tt = tc * 4 + j
            pb, pi, g = ps_trf.next()
            P.guard("tensor", g)
            tv = P.ev("tensor", s_tr, lambda e, o=pb, j=j: e.transpose(
                o[:, 0:64], rls[:, j * 128:(j + 1) * 128], ident_f[0:64, 0:64]))
            rt, ri, g2 = rl_t.next()
            P.guard("vector", g2)
            P.wait("vector", s_tr, tv)
            ev = P.ev("vector", s_ev, lambda e, o=rt, p=pb: e.tensor_copy(o[:, :], p[:, 0:64]))
            ps_trf.release(pi, s_ev, ev)
            P.wait("vector", s_ev, ev)
            k1 = P.ev("vector", s_r, lambda e, i=rt: e.max(m8[:, 0:8], i[:, :]))
            P.wait("vector", s_r, k1)
            k1 = P.ev("vector", s_r, lambda e: e.tensor_scalar(m8[:, 8:9], m8[:, 0:1], -1.0, None, ALU.mult))
            P.wait("scalar", s_r, k1)
            k2 = P.ev("scalar", s_g, lambda e, i=rt: e.activation(e8t[:, :], i[:, :], AF.Exp, bias=m8[:, 8:9]))
            P.wait("vector", s_g, k2)
            for fn in [
                lambda e, i=rt: e.tensor_scalar(i[:, :], i[:, :], m8[:, 7:8], None, ALU.is_ge),
                lambda e, i=rt: e.tensor_mul(e8t[:, :], e8t[:, :], i[:, :]),
                lambda e: e.tensor_reduce(m8[:, 9:10], e8t[:, :], axis=AX.X, op=ALU.add),
            ]:
                k3 = P.ev("vector", s_r, fn)
                P.wait("vector", s_r, k3)
            k3 = P.ev("vector", s_r, lambda e: e.reciprocal(m8[:, 10:11], m8[:, 9:10]))
            P.wait("vector", s_r, k3)
            k4 = P.ev("vector", s_r, lambda e, t=tt: e.tensor_scalar(
                alphab[t][:, :], e8t[:, :], m8[:, 10:11], None, ALU.mult))
            rl_t.release(ri, s_r, k4)
            P.wait("tensor", s_r, k4)
            pb2, pi2, g3 = ps_tr.next()
            P.guard("tensor", g3)
            tv = P.ev("tensor", s_tr, lambda e, o=pb2, t=tt: e.transpose(
                o[0:64, 0:128], alphab[t][:, :], ident_bf[:, :]))
            P.wait("vector", s_tr, tv)
            ev = P.ev("vector", s_ev, lambda e, p=pb2, t=tt: e.tensor_copy(
                alphaF[:, t * 128:(t + 1) * 128], p[0:64, 0:128]))
            ps_tr.release(pi2, s_ev, ev)
        xsl.release(fi, s_mm, s_mm.n)
    alpha_done = s_ev.n

    P.wait("tensor", s_ev, alpha_done)
    P.wait("tensor", s_r, s_r.n)
    for tt in range(NT):
        for part, w in ((0, 512), (512, 512)):
            P.mm(ps_b[:, part:part + w], alphab[tt][:, :], x_tok1[tt][:, part:part + w],
                 start=(tt == 0), stop=(tt == NT - 1))
        P.mm(ps_d[:, 0:1], alphab[tt][:, :], x_tok1[tt][:, 1024:1025],
             start=(tt == 0), stop=(tt == NT - 1),
             sem=(s_mm if tt == NT - 1 else None))
    P.wait("vector", s_mm, s_mm.n)
    for fn in [
        lambda e: e.tensor_scalar(lsc[:, 0:1], ps_d[0:64, 0:1], 1e-8, None, ALU.add),
    ]:
        vv = P.ev("vector", s_r, fn)
        P.wait("vector", s_r, vv)
    vv = P.ev("vector", s_r, lambda e: e.reciprocal(lsc[:, 1:2], lsc[:, 0:1]))
    P.wait("vector", s_r, vv)
    vv = P.ev("vector", s_r, lambda e: e.tensor_scalar(
        sb64[:, :], ps_b[0:64, 0:1024], lsc[:, 1:2], None, ALU.mult))
    P.wait("tensor", s_r, vv)
    for dc in range(ND):
        pb, pi, g = ps_tr.next()
        P.guard("tensor", g)
        tv = P.ev("tensor", s_tr, lambda e, o=pb, c=dc: e.transpose(
            o[:, 0:64], sb64[:, c * 128:(c + 1) * 128], ident_bf[0:64, 0:64]))
        P.wait("vector", s_tr, tv)
        ev = P.ev("vector", s_ev, lambda e, p=pb, c=dc: e.tensor_copy(sF[:, c * 64:(c + 1) * 64], p[:, 0:64]))
        ps_tr.release(pi, s_ev, ev)
    siF_done = s_ev.n

    def load_w3(src, sec):
        wb, wi, g = wbuf.next()
        last = None
        for e16 in range(16):
            if g is not None:
                P.guard("sync", g)
            if last is not None:
                P.wait("sync", s_cast, last)
            r0, c0 = (e16 // 2) * 128, (e16 % 2) * 512
            v = P.dma("sync", wstage[:, 0:512], src[r0:r0 + 128, sec * D + c0:sec * D + c0 + 512], s_ld)
            P.wait("scalar", s_ld, v)
            last = P.ev("scalar", s_cast, lambda e, d=wb, s8=e16 // 2, c0=c0:
                        e.copy(d[:, s8 * 1024 + c0: s8 * 1024 + c0 + 512], wstage[:, 0:512]))
        return wb, wi, last

    def load_row(src, sec, dst):
        P.wait("sync", s_cast, s_cast.n)
        v = P.dma("sync", wstage[0:1, 0:1024], src[:, sec * D:(sec + 1) * D], s_ld)
        P.wait("scalar", s_ld, v)
        return P.ev("scalar", s_cast, lambda e, d=dst: e.copy(d[:, :], wstage[0:1, 0:1024]))

    P.wait("tensor", s_ev, siF_done)
    P.wait("tensor", s_r, s_r.n)
    for sec in range(3):
        wb, wi, wr = load_w3(wih_in, sec)
        rv = load_row(bih_in, sec, birow)
        P.wait("tensor", s_cast, rv)
        P.wait("tensor", s_g, s_g.n)
        for dc in range(ND):
            for half in range(2):
                P.mm(ps_b[:, half * 512:(half + 1) * 512],
                     sF[:, dc * 64:(dc + 1) * 64],
                     wb[:, dc * 1024 + half * 512: dc * 1024 + (half + 1) * 512],
                     start=(dc == 0), stop=False)
        P.mm(ps_b[:, 0:512], ones_bf[:, 0:64], birow[:, 0:512], start=False, stop=True)
        gi_l = P.mm(ps_b[:, 512:1024], ones_bf[:, 0:64], birow[:, 512:1024], start=False, stop=True, sem=s_mm)
        wbuf.release(wi, s_mm, gi_l)
        wb2, wi2, wr2 = load_w3(whh_in, sec)
        rv2 = load_row(bhh_in, sec, bhrow)
        P.wait("tensor", s_cast, rv2)
        for dc in range(ND):
            for half in range(2):
                P.mm(ps_c[:, half * 512:(half + 1) * 512],
                     sinitT[:, dc * 64:(dc + 1) * 64],
                     wb2[:, dc * 1024 + half * 512: dc * 1024 + (half + 1) * 512],
                     start=(dc == 0), stop=False)
        P.mm(ps_c[:, 0:512], ones_bf[:, 0:64], bhrow[:, 0:512], start=False, stop=True)
        gh_l = P.mm(ps_c[:, 512:1024], ones_bf[:, 0:64], bhrow[:, 512:1024], start=False, stop=True, sem=s_mm)
        wbuf.release(wi2, s_mm, gh_l)
        P.wait("scalar", s_mm, gh_l)
        e1 = P.ev("scalar", s_g, lambda e: e.copy(srow[:, :], ps_c[0:64, :]))
        P.wait("vector", s_g, e1)
        if sec == 0:
            a1 = P.ev("vector", s_g, lambda e: e.tensor_add(srow[:, :], srow[:, :], ps_b[0:64, 0:1024]))
            P.wait("scalar", s_g, a1)
            a2 = P.ev("scalar", s_g, lambda e: e.activation(r_g[:, :], srow[:, :], AF.Sigmoid))
        elif sec == 1:
            a1 = P.ev("vector", s_g, lambda e: e.tensor_add(srow[:, :], srow[:, :], ps_b[0:64, 0:1024]))
            P.wait("scalar", s_g, a1)
            a2 = P.ev("scalar", s_g, lambda e: e.activation(z_g[:, :], srow[:, :], AF.Sigmoid))
        else:
            a1 = P.ev("vector", s_g, lambda e: e.tensor_mul(srow[:, :], srow[:, :], r_g[:, :]))
            P.wait("vector", s_g, a1)
            a1 = P.ev("vector", s_g, lambda e: e.tensor_add(srow[:, :], srow[:, :], ps_b[0:64, 0:1024]))
            P.wait("scalar", s_g, a1)
            a2 = P.ev("scalar", s_g, lambda e: e.activation(srow[:, :], srow[:, :], AF.Tanh))
        P.wait("tensor", s_g, a2)
        P.wait("vector", s_g, a2)
    for fn in [
        lambda e: e.tensor_mul(g2s[:, :], z_g[:, :], srow[:, :]),
        lambda e: e.tensor_mul(Snew[:, :], z_g[:, :], S0[:, :]),
        lambda e: e.tensor_add(Snew[:, :], Snew[:, :], srow[:, :]),
        lambda e: e.tensor_sub(Snew[:, :], Snew[:, :], g2s[:, :]),
    ]:
        vv = P.ev("vector", s_g, fn)
        P.wait("vector", s_g, vv)
    P.wait("scalar", s_g, s_g.n)
    cv = P.ev("scalar", s_g, lambda e: e.copy(sb64[:, :], Snew[:, :]))
    P.wait("tensor", s_g, cv)
    for dc in range(ND):
        pb, pi, g = ps_tr.next()
        P.guard("tensor", g)
        tv = P.ev("tensor", s_tr, lambda e, o=pb, c=dc: e.transpose(
            o[:, 0:64], sb64[:, c * 128:(c + 1) * 128], ident_bf[0:64, 0:64]))
        P.wait("vector", s_tr, tv)
        ev = P.ev("vector", s_ev, lambda e, p=pb, c=dc: e.tensor_copy(sF[:, c * 64:(c + 1) * 64], p[:, 0:64]))
        ps_tr.release(pi, s_ev, ev)
    SnF_done = s_ev.n

    P.wait("tensor", s_ev, SnF_done)
    def proj_small(w_src, b_off, b_w, out_sb, width):
        wb, wi, g = wbuf.next()
        last = None
        nblk = (width + 511) // 512
        for dc in range(ND):
            if g is not None:
                P.guard("sync", g)
            if last is not None:
                P.wait("sync", s_cast, last)
            v = P.dma("sync", wstage[:, 0:width], w_src[dc * 128:(dc + 1) * 128, :], s_ld)
            P.wait("scalar", s_ld, v)
            last = P.ev("scalar", s_cast, lambda e, d=wb, c=dc, w=width: e.copy(
                d[:, c * 1024:c * 1024 + w], wstage[:, 0:w]))
        P.wait("tensor", s_cast, last)
        P.wait("tensor", s_g, s_g.n)
        for dc in range(ND):
            for part in range(nblk):
                w0 = part * 512
                w1 = min(width, w0 + 512)
                P.mm(ps_c[:, w0:w1], sF[:, dc * 64:(dc + 1) * 64],
                     wb[:, dc * 1024 + w0: dc * 1024 + w1], start=(dc == 0), stop=False)
        for part in range(nblk):
            w0 = part * 512
            w1 = min(width, w0 + 512)
            P.mm(ps_c[:, w0:w1], ones_bf[:, 0:64], rows2[:, b_off + w0:b_off + w1],
                 start=False, stop=True, sem=(s_mm if w1 == width else None))
        wbuf.release(wi, s_mm, s_mm.n)
        P.wait("scalar", s_mm, s_mm.n)
        return P.ev("scalar", s_g, lambda e, o=out_sb, w=width: e.copy(o[:, :], ps_c[0:64, 0:w]))

    qv = proj_small(wq_in, 0, DQ, qs, DQ)
    P.wait("tensor", s_g, qv)
    kv = proj_small(wk_in, DQ, DQ, ks, DQ)
    P.wait("tensor", s_g, kv)
    vv2 = proj_small(wv_in, 512, D, vs, D)
    P.wait("tensor", s_g, vv2)
    for (src, dst) in ((qs, qT), (ks, kT)):
        for i2 in range(2):
            pb, pi, g = ps_tr.next()
            P.guard("tensor", g)
            tv = P.ev("tensor", s_tr, lambda e, o=pb, s=src, i=i2: e.transpose(
                o[:, 0:64], s[:, i * 128:(i + 1) * 128], ident_bf[0:64, 0:64]))
            P.wait("vector", s_tr, tv)
            ev = P.ev("vector", s_ev, lambda e, p=pb, d=dst, i=i2: e.tensor_copy(
                d[:, i * 64:(i + 1) * 64], p[:, 0:64]))
            ps_tr.release(pi, s_ev, ev)
    P.wait("tensor", s_ev, s_ev.n)
    P.wait("tensor", s_r, s_r.n)
    P.mm(ps_b[:, 0:64], qT[:, 0:64], kT[:, 0:64], start=True, stop=False)
    sc_l = P.mm(ps_b[:, 0:64], qT[:, 64:128], kT[:, 64:128], start=False, stop=True, sem=s_mm)
    P.wait("vector", s_mm, sc_l)
    k1 = P.ev("vector", s_r, lambda e: e.tensor_reduce(lsc[:, 2:3], ps_b[0:64, 0:64], axis=AX.X, op=ALU.max, negate=True))
    P.wait("vector", s_r, k1)
    k1 = P.ev("vector", s_r, lambda e: e.tensor_scalar(lsc[:, 3:4], lsc[:, 2:3], 1.0 / 16, None, ALU.mult))
    P.wait("scalar", s_r, k1)
    k2 = P.ev("scalar", s_g, lambda e: e.activation(escr[:, :], ps_b[0:64, 0:64], AF.Exp, bias=lsc[:, 3:4], scale=1.0 / 16))
    P.wait("vector", s_g, k2)
    k3 = P.ev("vector", s_r, lambda e: e.tensor_reduce(lsc[:, 4:5], escr[:, :], axis=AX.X, op=ALU.add))
    P.wait("vector", s_r, k3)
    k3 = P.ev("vector", s_r, lambda e: e.reciprocal(lsc[:, 5:6], lsc[:, 4:5]))
    P.wait("vector", s_r, k3)
    k4 = P.ev("vector", s_r, lambda e: e.tensor_scalar(pat[:, :], escr[:, :], lsc[:, 5:6], None, ALU.mult))
    P.wait("tensor", s_r, k4)
    pb, pi, g = ps_tr.next()
    P.guard("tensor", g)
    tv = P.ev("tensor", s_tr, lambda e, o=pb: e.transpose(o[0:64, 0:64], pat[:, :], ident_bf[0:64, 0:64]))
    P.wait("vector", s_tr, tv)
    ev = P.ev("vector", s_ev, lambda e, p=pb: e.tensor_copy(patT[:, :], p[0:64, 0:64]))
    ps_tr.release(pi, s_ev, ev)
    P.wait("tensor", s_ev, ev)
    P.wait("tensor", s_g, s_g.n)
    P.mm(ps_c[:, 0:512], patT[:, :], vs[:, 0:512], start=True, stop=True)
    cm_l = P.mm(ps_c[:, 512:1024], patT[:, :], vs[:, 512:1024], start=True, stop=True, sem=s_mm)
    P.wait("vector", s_mm, cm_l)
    a1 = P.ev("vector", s_g, lambda e: e.tensor_add(srow[:, :], Snew[:, :], ps_c[0:64, 0:1024]))
    P.wait("scalar", s_g, a1)
    sq1 = P.ev("scalar", s_ln, lambda e: e.activation(ps_sq[0:64, :], srow[:, 0:512], AF.Square))
    P.wait("vector", s_ln, sq1)
    r1 = P.ev("vector", s_ln, lambda e: e.tensor_reduce(lsc[:, 0:1], ps_sq[0:64, :], axis=AX.X, op=ALU.add))
    P.wait("scalar", s_ln, r1)
    sq2 = P.ev("scalar", s_ln, lambda e: e.activation(ps_sq[0:64, :], srow[:, 512:1024], AF.Square))
    P.wait("vector", s_ln, sq2)
    r2 = P.ev("vector", s_ln, lambda e: e.tensor_reduce(lsc[:, 4:5], ps_sq[0:64, :], axis=AX.X, op=ALU.add))
    nsv = P.ev("vector", s_ln, lambda e: e.tensor_reduce(lsc[:, 1:2], srow[:, :], axis=AX.X, op=ALU.add, negate=True))
    P.wait("vector", s_ln, nsv)
    for fn in [
        lambda e: e.tensor_add(lsc[:, 0:1], lsc[:, 0:1], lsc[:, 4:5]),
        lambda e: e.tensor_scalar(lsc[:, 2:3], lsc[:, 1:2], 1.0 / D, None, ALU.mult),
        lambda e: e.tensor_mul(lsc[:, 3:4], lsc[:, 2:3], lsc[:, 2:3]),
        lambda e: e.tensor_scalar(lsc[:, 4:5], lsc[:, 0:1], 1.0 / D, None, ALU.mult),
        lambda e: e.tensor_sub(lsc[:, 5:6], lsc[:, 4:5], lsc[:, 3:4]),
        lambda e: e.tensor_scalar(lsc[:, 6:7], lsc[:, 5:6], 1e-5, None, ALU.add),
    ]:
        vv = P.ev("vector", s_ln, fn)
        P.wait("vector", s_ln, vv)
    P.wait("scalar", s_ln, s_ln.n)
    qv2 = P.ev("scalar", s_ln, lambda e: e.activation(lsc[:, 7:8], lsc[:, 6:7], AF.Sqrt))
    P.wait("vector", s_ln, qv2)
    rv2 = P.ev("vector", s_ln, lambda e: e.reciprocal(lsc[:, 6:7], lsc[:, 7:8]))
    P.wait("vector", s_ln, rv2)
    ov = P.ev("vector", s_ln, lambda e: e.tensor_scalar(
        srow[:, :], srow[:, :], lsc[:, 2:3], lsc[:, 6:7], ALU.add, ALU.mult))
    P.wait("tensor", s_ln, ov)
    P.wait("tensor", s_g, s_g.n)
    P.mm(ps_c[:, 0:512], ones_bf[:, 0:64], rows2[:, 1536:2048], start=True, stop=True)
    b1m = P.mm(ps_c[:, 512:1024], ones_bf[:, 0:64], rows2[:, 2048:2560], start=True, stop=True, sem=s_mm)
    P.wait("vector", s_mm, b1m)
    m1 = P.ev("vector", s_ln, lambda e: e.tensor_mul(srow[:, :], srow[:, :], ps_c[0:64, 0:1024]))
    P.wait("tensor", s_ln, m1)
    P.mm(ps_c[:, 0:512], ones_bf[:, 0:64], rows2[:, 2560:3072], start=True, stop=True)
    b2m = P.mm(ps_c[:, 512:1024], ones_bf[:, 0:64], rows2[:, 3072:3584], start=True, stop=True, sem=s_mm)
    P.wait("vector", s_mm, b2m)
    m2 = P.ev("vector", s_ln, lambda e: e.tensor_add(srow[:, :], srow[:, :], ps_c[0:64, 0:1024]))
    P.wait("scalar", s_ln, m2)
    cb = P.ev("scalar", s_g, lambda e: e.copy(sb64[:, :], srow[:, :]))
    P.wait("tensor", s_g, cb)
    for dc in range(ND):
        pb, pi, g = ps_tr.next()
        P.guard("tensor", g)
        tv = P.ev("tensor", s_tr, lambda e, o=pb, c=dc: e.transpose(
            o[:, 0:64], sb64[:, c * 128:(c + 1) * 128], ident_bf[0:64, 0:64]))
        P.wait("vector", s_tr, tv)
        ev = P.ev("vector", s_ev, lambda e, p=pb, c=dc: e.tensor_copy(sF[:, c * 64:(c + 1) * 64], p[:, 0:64]))
        ps_tr.release(pi, s_ev, ev)
    yF_done = s_ev.n
    P.wait("tensor", s_ev, yF_done)
    sv_l = proj_small(wvp_in, 3584, D, svb, D)
    P.wait("tensor", s_g, sv_l)

    wb_o, wi_o, wr_o = load_w3(wo_in, 0)
    ctx = {"xr_rot": x_stage, "lnt": lnt, "ps_sq": ps_sq,
           "s_ld2": s_ld2, "s_ln": s_ln, "s_st": s_st}
    P.wait("tensor", s_cast, wr_o)
    for tc in range(NQC):
        last_g = None
        for dc in range(ND):
            pb, pi, g = ps_a.next()
            P.guard("tensor", g)
            P.mm(pb[:, :], svb[:, dc * 128:(dc + 1) * 128],
                 alphaF[:, tc * 512:(tc + 1) * 512], start=True, stop=True, sem=s_mm)
            P.wait("scalar", s_mm, s_mm.n)
            gv2 = P.ev("scalar", s_g, lambda e, o=gfb[dc], p=pb: e.copy(o[:, :], p[:, :]))
            ps_a.release(pi, s_g, gv2)
            last_g = gv2
        P.wait("tensor", s_g, last_g)
        for oc in range(ND):
            pb, pi, g = ps_a.next()
            P.guard("tensor", g)
            for dc in range(ND):
                P.mm(pb[:, :], wb_o[:, dc * 1024 + oc * 128: dc * 1024 + oc * 128 + 128],
                     gfb[dc][:, :], start=(dc == 0), stop=False)
            mv = P.mm(pb[:, :], rows2[:, 4608 + oc * 128: 4608 + oc * 128 + 128],
                      ones_bf[:, :], start=False, stop=True, sem=s_mm)
            P.wait("scalar", s_mm, mv)
            yb, yi, g2 = ytmp.next()
            P.guard("scalar", g2)
            cv = P.ev("scalar", s_g, lambda e, y=yb, p=pb: e.copy(y[:, :], p[:, :]))
            ps_a.release(pi, s_g, cv)
            P.wait("tensor", s_g, cv)
            lastt = None
            for j in range(4):
                pt_, pti, g3 = ps_tr.next()
                P.guard("tensor", g3)
                tv = P.ev("tensor", s_tr, lambda e, o=pt_, i=yb, j=j:
                          e.transpose(o[:, :], i[:, j * 128:(j + 1) * 128], ident_bf[:, :]))
                P.wait("vector", s_tr, tv)
                evv = P.ev("vector", s_ev, lambda e, o=y_blk[j], p=pt_, c=oc:
                           e.tensor_copy(o[:, c * 128:(c + 1) * 128], p[:, :]))
                ps_tr.release(pti, s_ev, evv)
                lastt = tv
            ytmp.release(yi, s_tr, lastt)
        P.wait("vector", s_ev, s_ev.n)
        for j in range(4):
            emit_postln(P, ctx, tc * 4 + j, y_blk[j][:, :], x_in, y_out)
    P.wait("gpsimd", s_st, s_st.n)
    return P


def _prep_l1_ffn_inputs(x_full, slot_init, w1, b1, w2, b2, tau, Wih, Whh, bih, bhh,
                        wq, bq, wk, bk, wv, bv, ln_g, ln_b, wvp, bvp, wo, bo):
    consts = host_consts()
    base = {
        "sinit_in": np.ascontiguousarray(slot_init),
        "w1_in": np.ascontiguousarray(w1),
        "b1_in": np.ascontiguousarray(b1)[:, None],
        "w2_in": np.ascontiguousarray(w2),
        "b2_in": np.ascontiguousarray(b2)[:, None],
        "tau_in": np.ascontiguousarray(tau).reshape(1, 1),
        "wih_in": np.ascontiguousarray(Wih.T),
        "whh_in": np.ascontiguousarray(Whh.T),
        "bih_in": np.ascontiguousarray(bih)[None, :],
        "bhh_in": np.ascontiguousarray(bhh)[None, :],
        "wq_in": np.ascontiguousarray(wq),
        "bq_in": np.ascontiguousarray(bq)[None, :],
        "wk_in": np.ascontiguousarray(wk),
        "bk_in": np.ascontiguousarray(bk)[None, :],
        "wv_in": np.ascontiguousarray(wv),
        "bv_in": np.ascontiguousarray(bv)[None, :],
        "lng_in": np.ascontiguousarray(ln_g)[None, :],
        "lnb_in": np.ascontiguousarray(ln_b)[None, :],
        "wvp_in": np.ascontiguousarray(wvp),
        "bvp_in": np.ascontiguousarray(bvp)[None, :],
        "wo_in": np.ascontiguousarray(wo),
        "bo_in": np.ascontiguousarray(bo)[None, :],
        "ident_in": consts["ident"],
    }
    in_maps = []
    for c in range(NCORES):
        m = dict(base)
        m["x_in"] = np.ascontiguousarray(x_full[c])
        in_maps.append(m)
    return in_maps


def run_l1_ffn(x_full, *args):
    in_maps = _prep_l1_ffn_inputs(x_full, *args)
    P = build_prog_l1_ffn()
    nc = P.finalize()
    res = run_bass_kernel_spmd(nc, in_maps, core_ids=list(range(NCORES)))
    return np.stack([r["y_out"] for r in res.results], axis=0)
